# version 03
# baseline (speedup 1.0000x reference)
import os
import numpy as np
import ml_dtypes
import concourse.bass as bass
import concourse.mybir as mybir
from concourse.bass_utils import run_bass_kernel_spmd

F32 = mybir.dt.float32
BF16 = mybir.dt.bfloat16
I32 = mybir.dt.int32
AF = mybir.ActivationFunctionType
ALU = mybir.AluOpType
AX = mybir.AxisListType

T = 8192
C = 1024
TB = (0, 3072, 5120, 6656, 8192)
R2B = 3840
SMAX = 4224
NQMAX = 3200

KDEBUG = int(os.environ.get("KDEBUG", "0"))
KTILES = int(os.environ.get("KTILES", "0"))
KSTOP = int(os.environ.get("KSTOP", "0"))
KSTOPA = int(os.environ.get("KSTOPA", "0"))
KSKIPR2 = int(os.environ.get("KSKIPR2", "0"))
KR2S = int(os.environ.get("KR2S", "0"))
KBODIES = [int(b) for b in os.environ.get("KBODIES", "0,1,2,3").split(",")]


def slice_cfg(s):
    T0, T1 = TB[s], TB[s + 1]
    Q0 = max(0, T0 - 128)
    Q1 = min(T, T1 + 128)
    P1 = min(T, T1 + 512)
    return dict(s=s, T0=T0, T1=T1, Q0=Q0, Q1=Q1, P1=P1,
                NB=P1 // 128, NU=P1 // 512, nq=Q1 - Q0,
                G0=Q0 // 512, G1=(Q1 - 1) // 512)


def r2_slots(u):
    out = []
    for j in range(8):
        out.append((j, 64, 8 * u + j))
    for j in range(4):
        out.append((8 + j, 128, 4 * u + j))
    for j in range(2):
        out.append((12 + j, 256, 2 * u + j))
    out.append((14, 512, u))
    for j, L in ((15, 1024), (16, 2048), (17, 4096), (18, 8192)):
        if (512 * (u + 1)) % L == 0:
            out.append((j, L, (512 * u) // L))
    return out


def set_r_index(NBmax=64, NUmax=16):
    r = np.full(SMAX, 10 ** 9, dtype=np.int64)
    real = np.zeros(SMAX, dtype=bool)
    for b in range(NBmax):
        off, pos = 60 * b, 0
        for L, cnt in ((4, 32), (8, 16), (16, 8), (32, 4)):
            for i in range(cnt):
                r[off + pos] = 128 * b + (32 * b // (L // 4)) * 0 + i * L + L - 1
                real[off + pos] = True
                pos += 1
    for u in range(NUmax):
        off = R2B + 19 * u
        for j, L, i in r2_slots(u):
            r[off + j] = i * L + L - 1
            real[off + j] = True
    return r, real


def host_masks():
    r, _ = set_r_index()
    data = np.ones((16, 5, 128, 512), dtype=np.float32)
    meta = []
    for G in range(16):
        chunks = list(range((240 * G) // 128, (240 * (G + 1) + 127) // 128))
        chunks += list(range((R2B + 19 * G) // 128, (R2B + 19 * (G + 1) + 127) // 128))
        assert len(chunks) <= 5
        meta.append(chunks)
        tq = 512 * G + np.arange(512)
        for k, cc in enumerate(chunks):
            rows = np.arange(128 * cc, 128 * cc + 128)
            lo, hi = ((240 * G, 240 * (G + 1)) if cc < 30
                      else (R2B + 19 * G, R2B + 19 * (G + 1)))
            m = np.ones((128, 512), dtype=np.float32)
            bnd = (rows >= lo) & (rows < hi)
            m[bnd] = (r[rows][bnd][:, None] <= tq[None, :]).astype(np.float32)
            m[rows >= hi] = 0.0
            data[G, k] = (m - 1.0) * 4000.0
    return data.astype(ml_dtypes.bfloat16), meta


MASK_DATA, MASK_META = host_masks()


def host_mmask():
    r, real = set_r_index()
    mm = np.zeros((16, 128, 28), dtype=np.float32)
    for G in range(16):
        for j in range(7):
            row = R2B + 19 * G + 12 + j
            for ti in range(4):
                tq = 512 * G + 128 * ti + np.arange(128)
                mm[G, :, ti * 7 + j] = (real[row] & (r[row] <= tq)).astype(np.float32)
    return mm.astype(ml_dtypes.bfloat16)


MMASK = host_mmask()

ENGS = ("sync", "tensor", "vector", "scalar", "gpsimd")


class Plan:
    def __init__(self):
        self.ops = {e: [] for e in ENGS}
        self.sem = {}

    def cur(self, sem):
        return self.sem.get(sem, 0)

    def bump(self, sem, n):
        self.sem[sem] = self.cur(sem) + n
        return self.sem[sem]

    ENGSEM = {"vector": "V", "scalar": "S", "gpsimd": "G"}

    def op(self, eng, fn, inc=None, dma=False):
        incs = [inc] if isinstance(inc, tuple) else list(inc or [])
        dma = dma or any(s.startswith("d") for s, _ in incs)
        auto = eng in self.ENGSEM and not dma
        if auto:
            assert not incs, "compute ops on V/S/G auto-inc their engine sem"
            self.bump(self.ENGSEM[eng], 1)
        for s, n in incs:
            self.bump(s, n)
        self.ops[eng].append(("op", fn, incs, dma, auto))

    def wait(self, eng, sem, val=None):
        v = self.cur(sem) if val is None else val
        self.ops[eng].append(("wait", sem, v))
        return v

    def barrier(self, sems=("T", "V", "S", "G", "dK", "dV", "dVT", "dQ", "dG", "dW", "dZ", "dX0", "dX1")):
        for e in ("tensor", "vector", "scalar", "gpsimd"):
            for s in sems:
                if self.cur(s):
                    self.wait(e, s)

    def replay(self, name, eng, semh):
        selfsem = self.ENGSEM.get(name)
        cnt = 0
        for it in self.ops[name]:
            if it[0] == "wait":
                eng.wait_ge(semh[it[1]], it[2])
            else:
                _, fn, incs, dma, auto = it
                if selfsem and cnt > 0:
                    eng.wait_ge(semh[selfsem], cnt)
                inst = fn(eng)
                if auto:
                    inst.then_inc(semh[selfsem], 1)
                    cnt += 1
                for s, n in incs:
                    inst.then_inc(semh[s], n)


class Arena:

    def __init__(self, handle, nbytes):
        self.h = handle
        self.nbytes = nbytes
        self.off = 0
        self.marks = []

    def alloc(self, dtype, elems, parts=128):
        esz = 4 if dtype == F32 else 2
        self.off = (self.off + esz - 1) // esz * esz
        a = self.off
        self.off += elems * esz
        assert self.off <= self.nbytes, f"arena overflow {self.off} > {self.nbytes}"
        if dtype == F32:
            return self.h.bitcast(F32).ap()[0:parts, a // 4: a // 4 + elems]
        return self.h.bitcast(BF16).ap()[0:parts, a // 2: a // 2 + elems]

    def mark(self):
        self.marks.append(self.off)

    def release(self):
        self.off = self.marks.pop()


def r3o(view, off, inner, count, stride):
    assert off + inner <= stride
    return view[:, 0:stride * count].rearrange("p (b w) -> p b w", w=stride)[:, :, off:off + inner]


def emit_qkv(P, cfg, t):
    NB, NU, P1, Q0, Q1 = cfg["NB"], cfg["NU"], cfg["P1"], cfg["Q0"], cfg["Q1"]
    ar = t["ar"]
    pw0, pw1 = t["pw0"], t["pw1"]
    ar.mark()

    watf = ar.alloc(BF16, 8 * 3072)
    xstage = ar.alloc(BF16, 2 * 4 * 1024)
    xT = ar.alloc(BF16, 8 * 512)
    s2b = ar.alloc(F32, 256)
    s4b = ar.alloc(F32, 128)
    s8b = ar.alloc(F32, 64)
    s16b = ar.alloc(F32, 32)
    l32 = ar.alloc(F32, 16 * 256)
    stg = ar.alloc(BF16, 2 * 240)
    vst = ar.alloc(BF16, 2 * 240)
    vtp = ar.alloc(BF16, 2 * 256)
    kt8 = ar.alloc(BF16, 8 * 512)
    vt8 = ar.alloc(BF16, 512)
    prodb = ar.alloc(BF16, 512)
    qstg = ar.alloc(BF16, 512)
    attb = t["att_tail"]

    ntile = P1 // 512
    if KTILES:
        ntile = min(ntile, KTILES)
    own_lo, own_hi = Q0 // 512, (Q1 + 511) // 512

    zb = ar.alloc(BF16, 512 * 2)
    P.op("vector", lambda e: e.memset(zb, 0.0))
    P.wait("gpsimd", "V")
    flatK = t["d_KsF"]
    flatV = t["d_VsF"]
    nz = (16 * 64 * SMAX + 128 * 1024 - 1) // (128 * 1024)
    for z in range(nz):
        n = min(128 * 1024, 16 * 64 * SMAX - z * 128 * 1024)
        P.op("gpsimd", lambda e, z=z, n=n, flatK=flatK: e.dma_start(
            out=flatK[z * 128 * 1024: z * 128 * 1024 + n].rearrange("(p w) -> p w", p=128),
            in_=zb[:, 0:n // 128]), inc=("dZ", 16), dma=True)
        P.op("gpsimd", lambda e, z=z, n=n, flatV=flatV: e.dma_start(
            out=flatV[z * 128 * 1024: z * 128 * 1024 + n].rearrange("(p w) -> p w", p=128),
            in_=zb[:, 0:n // 128]), inc=("dZ", 16), dma=True)
    P.wait("gpsimd", "dZ")
    if KSTOP == 1:
        ar.release()
        return

    for cc in range(8):
        P.op("gpsimd", lambda e, cc=cc: e.dma_start(
            out=watf[:, cc * 3072:(cc + 1) * 3072],
            in_=t["wattnT"][128 * cc:128 * (cc + 1), :]), inc=("dW", 16))
    P.op("gpsimd", lambda e: e.dma_start(out=t["ident"].ap()[:, :], in_=t["identf"]), inc=("dW", 16))
    P.op("gpsimd", lambda e: e.dma_start(out=t["bones"].ap()[:, :], in_=t["bonesf"]), inc=("dW", 16))
    P.op("gpsimd", lambda e: e.dma_start(out=t["vones"], in_=t["vonesd"]), inc=("dW", 16))
    P.op("gpsimd", lambda e: e.dma_start(out=t["eind4"].ap()[:, :], in_=t["eind4f"]), inc=("dW", 16))
    P.op("gpsimd", lambda e: e.dma_start(out=t["eind44"].ap()[:, :], in_=t["eind44f"]), inc=("dW", 16))
    P.op("gpsimd", lambda e: e.dma_start(out=t["identf32"].ap()[:, :], in_=t["identf"]), inc=("dW", 16))
    P.wait("tensor", "dW")
    P.wait("vector", "dW")
    P.wait("scalar", "dW")
    if KSTOP == 2:
        ar.release()
        return

    for ti in range(ntile):
        is_own = own_lo <= ti < own_hi
        tok0 = 512 * ti
        oa = max(0, Q0 - tok0)
        ow = max(0, min(tok0 + 512, Q1) - max(tok0, Q0))
        xs = xstage[:, (ti % 2) * 4096:(ti % 2) * 4096 + 4096]
        if ti >= 2:
            P.wait("gpsimd", "T", t["xfree"][ti % 2])
        dx = "dX%d" % (ti % 2)
        for st in range(4):
            P.op("gpsimd", lambda e, st=st, tok0=tok0, xs=xs: e.dma_start(
                out=xs[:, st * 1024:(st + 1) * 1024],
                in_=t["xb"][tok0 + 128 * st: tok0 + 128 * (st + 1), :]), inc=(dx, 16), dma=True)
        P.wait("tensor", dx)

        for st in range(4):
            pb = pw0[:, (st % 2) * 1024:(st % 2) * 1024 + 1024]
            if ti > 0 or st >= 2:
                P.wait("tensor", "S", t["xtS"][st % 2])
            for ccc in range(8):
                P.op("tensor", lambda e, xs=xs, st=st, ccc=ccc, pb=pb: e.matmul(
                    out=pb[:, (ccc % 4) * 128 + (ccc // 4) * 512:(ccc % 4) * 128 + (ccc // 4) * 512 + 128],
                    lhsT=xs[:, st * 1024 + ccc * 128: st * 1024 + (ccc + 1) * 128],
                    rhs=t["ident"].ap()[:, :], start=True, stop=True),
                    inc=("T", 1) if ccc == 7 else None)
            P.wait("scalar", "T")
            for half in range(2):
                P.op("scalar", lambda e, pb=pb, half=half, st=st: e.copy(
                    out=r3o(xT[:, half * 4 * 512:], st * 128, 128, 4, 512),
                    in_=pb[:, half * 512: half * 512 + 512].rearrange("p (b w) -> p b w", w=128)),
                    )
            t["xtS"][st % 2] = P.cur("S")
        t["xfree"][ti % 2] = P.cur("T")
        P.wait("tensor", "S")
        if KSTOP == 3:
            continue

        fchunks = list(range(8, 24)) + (list(range(0, 8)) if is_own and KSTOP != 4 else [])
        if KSTOP == 5:
            fchunks = fchunks[:2]
        if KSTOP == 6:
            fchunks = [8, 16]
        if KSTOP == 7:
            fchunks = list(range(8, 12))
        if KSTOP == 8:
            fchunks = list(range(8, 24))[:8]
        for j, fc in enumerate(fchunks):
            slot = j % 2
            bank = pw1[:, slot * 512: slot * 512 + 512]
            P.wait("tensor", "V", t["qkvV"][slot])
            P.wait("tensor", "S", t["qkvS"][slot])
            for ccc in range(8):
                P.op("tensor", lambda e, fc=fc, ccc=ccc, bank=bank: e.matmul(
                    out=bank, lhsT=watf[:, ccc * 3072 + fc * 128: ccc * 3072 + fc * 128 + 128],
                    rhs=xT[:, ccc * 512:(ccc + 1) * 512],
                    start=(ccc == 0), stop=(ccc == 7)), inc=("T", 1) if ccc == 7 else None)
            tsig = P.cur("T")

            if fc >= 8:
                is_v = fc >= 16
                pr = fc - 16 if is_v else fc - 8
                stv = vst if is_v else stg
                sbuf_stg = stv[:, (ti % 2) * 240:(ti % 2) * 240 + 240]
                P.wait("vector", "T", tsig)
                if is_v:
                    P.wait("vector", "T", t["stgF"][(ti % 2, 1)])
                else:
                    P.wait("vector", "dK")
                P.op("vector", lambda e, bank=bank: e.tensor_reduce(
                    out=s4b, in_=bank.rearrange("p (b w) -> p b w", w=4),
                    axis=AX.X, op=ALU.add))
                P.op("vector", lambda e, sbuf_stg=sbuf_stg: e.tensor_copy(
                    out=r3o(sbuf_stg, 0, 32, 4, 60), in_=s4b.rearrange("p (b w) -> p b w", w=32)))
                P.op("vector", lambda e: e.tensor_tensor(
                    out=s8b, in0=s4b[:, 0:128:2], in1=s4b[:, 1:128:2], op=ALU.add))
                P.op("vector", lambda e, sbuf_stg=sbuf_stg: e.tensor_copy(
                    out=r3o(sbuf_stg, 32, 16, 4, 60), in_=s8b.rearrange("p (b w) -> p b w", w=16)))
                P.op("vector", lambda e: e.tensor_tensor(
                    out=s16b, in0=s8b[:, 0:64:2], in1=s8b[:, 1:64:2], op=ALU.add))
                P.op("vector", lambda e, sbuf_stg=sbuf_stg: e.tensor_copy(
                    out=r3o(sbuf_stg, 48, 8, 4, 60), in_=s16b.rearrange("p (b w) -> p b w", w=8)))
                kvi = pr + (8 if is_v else 0)
                P.op("vector", lambda e, kvi=kvi, ti=ti: e.tensor_tensor(
                    out=l32[:, kvi * 256 + ti * 16: kvi * 256 + ti * 16 + 16],
                    in0=s16b[:, 0:32:2], in1=s16b[:, 1:32:2], op=ALU.add))
                P.op("vector", lambda e, sbuf_stg=sbuf_stg, kvi=kvi, ti=ti: e.tensor_copy(
                    out=r3o(sbuf_stg, 56, 4, 4, 60),
                    in_=l32[:, kvi * 256 + ti * 16: kvi * 256 + ti * 16 + 16].rearrange("p (b w) -> p b w", w=4)))
                t["qkvV"][slot] = P.cur("V")
                vsig_stage = P.cur("V")

                if is_own:
                    P.wait("scalar", "T", tsig)
                    P.wait("scalar", "V", vsig_stage)
                    P.wait("scalar", "V", t["tailV"][slot])
                    dst = t["kvb"][:, slot * 512: slot * 512 + 512]
                    P.op("scalar", lambda e, bank=bank, dst=dst: e.copy(out=dst, in_=bank))
                    t["qkvS"][slot] = P.cur("S")
                    ssig = P.cur("S")
                    P.wait("vector", "S", ssig)
                    if is_v:
                        P.wait("vector", "dVT")
                    tl = kt8[:, pr * 512:(pr + 1) * 512] if not is_v else vt8
                    P.op("vector", lambda e, tl=tl, dst=dst: e.tensor_copy(out=tl, in_=dst))
                    for m in (1, 2, 3):
                        P.op("vector", lambda e, tl=tl, dst=dst, m=m: e.tensor_tensor(
                            out=r3o(tl, m, 4 - m, 128, 4),
                            in0=r3o(tl, m, 4 - m, 128, 4),
                            in1=r3o(dst, 0, 4 - m, 128, 4), op=ALU.add),
                            dma=False)
                    t["tailV"][slot] = P.cur("V")
                    if is_v:
                        vs = P.cur("V")
                        P.wait("gpsimd", "V", vs)
                        P.op("gpsimd", lambda e, pr=pr, tok0=tok0, oa=oa, ow=ow: e.dma_start(
                            out=t["d_vtailT"][128 * pr:128 * (pr + 1), tok0 + oa - Q0: tok0 + oa - Q0 + ow],
                            in_=vt8[:, oa:oa + ow]), inc=("dVT", 16))
                        t["vt8F"] = P.cur("dVT")
                else:
                    t["qkvS"][slot] = P.cur("S")

                if is_v and KSTOP == 9:
                    pass
                elif is_v:
                    P.wait("tensor", "V", vsig_stage)
                    P.wait("tensor", "S", t["vtpS"])
                    P.wait("tensor", "S", t["attS"])
                    for h2 in range(2):
                        ptv = pw1[:, 1024 + h2 * 512:1536 + h2 * 512]
                        for cchunk in range(2):
                            w = 128 if cchunk == 0 else 112
                            P.op("tensor", lambda e, sbuf_stg=sbuf_stg, h2=h2, cchunk=cchunk, w=w, ptv=ptv: e.matmul(
                                out=ptv[0:w, cchunk * 64:cchunk * 64 + 64],
                                lhsT=sbuf_stg[h2 * 64:(h2 + 1) * 64, cchunk * 128: cchunk * 128 + w],
                                rhs=t["ident"].ap()[h2 * 64:(h2 + 1) * 64, h2 * 64:(h2 + 1) * 64],
                                start=True, stop=True),
                                inc=("T", 1) if (h2 == 1 and cchunk == 1) else None)
                    ttv = P.cur("T")
                    t["stgF"][(ti % 2, 1)] = ttv
                    if KSTOP == 10:
                        continue
                    P.wait("scalar", "T", ttv)
                    P.wait("scalar", "dV")
                    vtpb = vtp[:, (ti % 2) * 256:(ti % 2) * 256 + 256]
                    for h2 in range(2):
                        ptv = pw1[:, 1024 + h2 * 512:1536 + h2 * 512]
                        P.op("scalar", lambda e, ptv=ptv, vtpb=vtpb, h2=h2: e.copy(
                            out=vtpb[:, h2 * 128:h2 * 128 + 64], in_=ptv[:, 0:64]))
                        P.op("scalar", lambda e, ptv=ptv, vtpb=vtpb, h2=h2: e.copy(
                            out=vtpb[0:112, h2 * 128 + 64:h2 * 128 + 128], in_=ptv[0:112, 64:128]))
                    t["vtpS"] = P.cur("S")
                    svt = P.cur("S")
                    if KSTOP == 11:
                        continue
                    P.wait("gpsimd", "S", svt)
                    for h2 in range(2):
                        hh = 2 * pr + h2
                        P.op("gpsimd", lambda e, hh=hh, h2=h2, ti=ti, vtpb=vtpb: e.dma_start(
                            out=t["d_Vs"][hh, 240 * ti: 240 * ti + 128, :],
                            in_=vtpb[:, h2 * 128: h2 * 128 + 64]), inc=("dV", 16))
                        P.op("gpsimd", lambda e, hh=hh, h2=h2, ti=ti, vtpb=vtpb: e.dma_start(
                            out=t["d_Vs"][hh, 240 * ti + 128: 240 * ti + 240, :],
                            in_=vtpb[0:112, h2 * 128 + 64: h2 * 128 + 128]), inc=("dV", 16))
                    t["vtpF"][ti % 2] = P.cur("dV")
                else:
                    P.wait("gpsimd", "V", vsig_stage)
                    for h2 in range(2):
                        hh = 2 * pr + h2
                        P.op("gpsimd", lambda e, hh=hh, h2=h2, ti=ti, sbuf_stg=sbuf_stg: e.dma_start(
                            out=t["d_Ks"][hh, :, 240 * ti: 240 * ti + 240],
                            in_=sbuf_stg[h2 * 64:(h2 + 1) * 64, :]), inc=("dK", 16))
                    t["stgF"][(ti % 2, 0)] = P.cur("dK")
            else:
                pr = fc
                P.wait("scalar", "T", tsig)
                P.wait("scalar", "V", t["prodV"])
                P.wait("scalar", "dQ")
                P.op("scalar", lambda e, bank=bank: e.copy(out=qstg, in_=bank))
                t["qkvS"][slot] = P.cur("S")
                t["qkvV"][slot] = P.cur("V")
                sq = P.cur("S")
                P.wait("gpsimd", "S", sq)
                P.op("gpsimd", lambda e, pr=pr, tok0=tok0, oa=oa, ow=ow: e.dma_start(
                    out=t["d_qT"][128 * pr:128 * (pr + 1), tok0 + oa - Q0: tok0 + oa - Q0 + ow],
                    in_=qstg[:, oa:oa + ow]), inc=("dQ", 16))
                t["qF"] = P.cur("dQ")
                P.wait("vector", "S", sq)
                P.wait("vector", "T", t["attT"])
                P.op("vector", lambda e, pr=pr: e.tensor_tensor(
                    out=prodb, in0=qstg, in1=kt8[:, pr * 512:(pr + 1) * 512], op=ALU.mult))
                pv = P.cur("V")
                t["prodV"] = pv
                P.wait("tensor", "V", pv)
                if pr == 0:
                    P.wait("tensor", "S", t["attS"])
                    P.wait("tensor", "S", t["vtpS"])
                P.op("tensor", lambda e, pr=pr, oa=oa, ow=ow: e.matmul(
                    out=pw1[0:16, 1536:1536 + ow], lhsT=t["bones"].ap()[:, pr * 16:(pr + 1) * 16],
                    rhs=prodb[:, oa:oa + ow], start=(pr == 0), stop=(pr == 7),
                    skip_group_check=True), inc=("T", 1))
                t["attT"] = P.cur("T")
                if pr == 7:
                    ts2 = P.cur("T")
                    P.wait("scalar", "T", ts2)
                    P.op("scalar", lambda e, tok0=tok0, oa=oa, ow=ow: e.copy(
                        out=attb.ap()[:, tok0 + oa - Q0: tok0 + oa - Q0 + ow],
                        in_=pw1[0:16, 1536:1536 + ow]))
                    t["attS"] = P.cur("S")

    if KSKIPR2:
        P.op("gpsimd", lambda e: e.engine_nop())
        ar.release()
        return
    r2w = 19 * NU
    r2k = ar.alloc(BF16, 2 * 304)
    s64 = ar.alloc(F32, 128)
    s128 = ar.alloc(F32, 64)
    s256 = ar.alloc(F32, 32)
    s512 = ar.alloc(F32, 16)
    s1024 = ar.alloc(F32, 8)
    s2048 = ar.alloc(F32, 4)
    s4096 = ar.alloc(F32, 2)
    s8192 = ar.alloc(F32, 1)
    for kvi in range(16):
        is_v = kvi >= 8
        pr = kvi - 8 if is_v else kvi
        P.wait("vector", "dK")
        P.wait("vector", "dV")
        P.wait("vector", "T")
        base = l32[:, kvi * 256: kvi * 256 + NU * 16]
        rstg = r2k[:, (0 if not is_v else 304):(304 if not is_v else 608)][:, 0:r2w]
        P.op("vector", lambda e, rstg=rstg: e.memset(rstg, 0.0))
        n64 = NU * 8
        P.op("vector", lambda e, base=base, n64=n64: e.tensor_tensor(
            out=s64[:, 0:n64], in0=base[:, 0:2 * n64:2], in1=base[:, 1:2 * n64:2], op=ALU.add))
        P.op("vector", lambda e, rstg=rstg, n64=n64, NU=NU: e.tensor_copy(
            out=r3o(rstg, 0, 8, NU, 19), in_=s64[:, 0:n64].rearrange("p (b w) -> p b w", w=8)))
        if KR2S == 1:
            continue
        lv = [(s64, s128, 4, 8), (s128, s256, 2, 12), (s256, s512, 1, 14)]
        for src, dst, per, off in lv:
            n = NU * per
            P.op("vector", lambda e, src=src, dst=dst, n=n: e.tensor_tensor(
                out=dst[:, 0:n], in0=src[:, 0:2 * n:2], in1=src[:, 1:2 * n:2], op=ALU.add))
            P.op("vector", lambda e, rstg=rstg, dst=dst, n=n, per=per, off=off, NU=NU: e.tensor_copy(
                out=r3o(rstg, off, per, NU, 19), in_=dst[:, 0:n].rearrange("p (b w) -> p b w", w=per)))
        if KR2S == 2:
            continue
        hi = [(s512, s1024, 34, 1024, 38), (s1024, s2048, 73, 2048, 76),
              (s2048, s4096, 150, 4096, 152), (s4096, s8192, 303, 8192, 304)]
        for src, dst, off, L, stride in hi:
            n = P1 // L
            if n == 0:
                continue
            P.op("vector", lambda e, src=src, dst=dst, n=n: e.tensor_tensor(
                out=dst[:, 0:n], in0=src[:, 0:2 * n:2], in1=src[:, 1:2 * n:2], op=ALU.add))
            cnt = n
            P.op("vector", lambda e, rstg=rstg, dst=dst, cnt=cnt, off=off, stride=stride: e.tensor_copy(
                out=r3o(rstg, off, 1, cnt, stride), in_=dst[:, 0:cnt].rearrange("p (b w) -> p b w", w=1)))
        vs = P.cur("V")
        if KR2S == 3:
            continue
        if KR2S == 5 and not is_v:
            continue
        if KR2S == 4 and is_v:
            continue
        if not is_v:
            P.wait("gpsimd", "V", vs)
            for h2 in range(2):
                hh = 2 * pr + h2
                P.op("gpsimd", lambda e, hh=hh, h2=h2, rstg=rstg, r2w=r2w: e.dma_start(
                    out=t["d_Ks"][hh, :, R2B: R2B + r2w], in_=rstg[h2 * 64:(h2 + 1) * 64, :]),
                    inc=("dK", 16))
        else:
            ptv = pw1[:, 1024:1536]
            P.wait("tensor", "V", vs)
            P.wait("tensor", "S", t["vtpS"])
            P.wait("tensor", "S")
            nch = (r2w + 127) // 128
            for h2 in range(2):
                pb2 = pw1[:, 1024 + h2 * 512: 1536 + h2 * 512]
                for cchunk in range(nch):
                    w = min(128, r2w - cchunk * 128)
                    P.op("tensor", lambda e, rstg=rstg, h2=h2, cchunk=cchunk, w=w, pb2=pb2: e.matmul(
                        out=pb2[:, cchunk * 64: cchunk * 64 + 64][0:w, :],
                        lhsT=rstg[h2 * 64:(h2 + 1) * 64, cchunk * 128: cchunk * 128 + w],
                        rhs=t["ident"].ap()[h2 * 64:(h2 + 1) * 64, h2 * 64:(h2 + 1) * 64],
                        start=True, stop=True),
                        inc=("T", 1) if (h2 == 1 and cchunk == nch - 1) else None)
            tt = P.cur("T")
            if KR2S == 7:
                continue
            P.wait("scalar", "T", tt)
            P.wait("scalar", "dV")
            for h2 in range(2):
                pb2 = pw1[:, 1024 + h2 * 512: 1536 + h2 * 512]
                for cchunk in range(nch):
                    w = min(128, r2w - cchunk * 128)
                    P.op("scalar", lambda e, pb2=pb2, h2=h2, cchunk=cchunk, w=w, nch=nch: e.copy(
                        out=vtp[0:w, (h2 * nch + cchunk) * 64:(h2 * nch + cchunk) * 64 + 64],
                        in_=pb2[0:w, cchunk * 64: cchunk * 64 + 64]))
            t["vtpS"] = P.cur("S")
            ss = P.cur("S")
            if KR2S == 6:
                continue
            P.wait("gpsimd", "S", ss)
            for h2 in range(2):
                hh = 2 * pr + h2
                for cchunk in range(nch):
                    w = min(128, r2w - cchunk * 128)
                    P.op("gpsimd", lambda e, hh=hh, h2=h2, cchunk=cchunk, w=w, nch=nch: e.dma_start(
                        out=t["d_Vs"][hh, R2B + cchunk * 128: R2B + cchunk * 128 + w, :],
                        in_=vtp[0:w, (h2 * nch + cchunk) * 64:(h2 * nch + cchunk) * 64 + 64]),
                        inc=("dV", 16))
    P.op("gpsimd", lambda e: e.engine_nop())
    ar.release()




def ap_view(base_ap, extra_elem_off, dims, p0=0, np_=None):
    import dataclasses
    pitch, pcount = base_ap.ap[0]
    if np_ is None:
        np_ = pcount - p0
    return dataclasses.replace(
        base_ap, offset=base_ap.offset + extra_elem_off + p0 * pitch,
        ap=type(base_ap.ap)([[pitch, np_]] + [[s, c] for s, c in dims]))


def emit_attention(P, cfg, t):
    P.barrier()
    NB, NU, P1, Q0, Q1, G0, G1 = (cfg["NB"], cfg["NU"], cfg["P1"], cfg["Q0"],
                                  cfg["Q1"], cfg["G0"], cfg["G1"])
    nq = cfg["nq"]
    NG = G1 - G0 + 1
    ar = t["ar"]
    pw0, pw1 = t["pw0"], t["pw1"]
    ar.mark()

    ch1max = (60 * NB + 127) // 128
    ch2max = (19 * NU + 127) // 128
    nch = ch1max + ch2max
    NCOLK = 128 * nch

    Kq = ar.alloc(BF16, 4 * NCOLK)
    Vq = ar.alloc(BF16, 4 * nch * 65)
    qTq = ar.alloc(BF16, 4 * nq)
    vtailq = ar.alloc(BF16, 2 * nq)
    Abuf = ar.alloc(BF16, 3 * 1024)
    maskb = ar.alloc(BF16, 5 * 512)
    attq = ar.alloc(F32, nq, parts=4)
    aexpq = ar.alloc(F32, nq, parts=4)
    cq4 = ar.alloc(F32, nq, parts=4)
    ccols = ar.alloc(BF16, 16)
    mmb = ar.alloc(BF16, 28)
    cbias = ar.alloc(F32, 128)
    accS = ar.alloc(BF16, 2 * 544)
    vtts = ar.alloc(BF16, 512)
    atxs = ar.alloc(F32, 16)
    zs = ar.alloc(F32, 16)
    rzs = ar.alloc(F32, 16)
    t["qTq_dbg"] = qTq
    t["cq4_dbg"] = cq4
    t["Vq_dbg"] = Vq
    t["Vq_cols"] = 4 * nch * 65
    t["vtailq_dbg"], t["vtailq_cols"] = vtailq, 2 * nq
    if KDEBUG:
        zall = ar.alloc(F32, (nq // 128) * 4)
        atall = ar.alloc(F32, (nq // 128) * 4)
        t["zall_dbg"], t["atall_dbg"], t["zall_cols"] = zall, atall, (nq // 128) * 4
        vtall = ar.alloc(BF16, min(NG, 5) * 512)
        t["vtall_dbg"], t["vtall_cols"] = vtall, min(NG, 5) * 512
    ntq = nq // 128
    attn_out = ar.alloc(BF16, ntq * 1024)
    t["attn_out"] = attn_out
    t["attn_ntq"] = ntq
    if KDEBUG:
        P.op("vector", lambda e: e.memset(attn_out, -777.0))

    def lchunks(G):
        return (list(range(0, (240 * (G + 1) + 127) // 128)) +
                list(range(ch1max, ch1max + (19 * (G + 1) + 127) // 128)))

    def vones_col(c):
        return c if c < ch1max else 30 + (c - ch1max)

    def groups():
        for G in range(G0, G1 + 1):
            t0 = max(Q0 // 128, 4 * G)
            t1 = min(Q1 // 128, 4 * G + 4)
            yield G, t0, t1 - t0

    slotp = [pw0[:, 0:1024], pw0[:, 1024:2048], pw1[:, 0:1024]]
    accp = [pw1[:, 1024:1536], pw1[:, 1536:2048]]
    st_slotS = [0, 0, 0]
    st_slotV = [0, 0, 0]
    st_AbufT = [0, 0, 0]
    st_accS = [0, 0]
    rot = [0]

    for hq in range(4):
        heads = [4 * hq + j for j in range(4)]
        P.wait("gpsimd", "T")
        P.wait("gpsimd", "V")
        for hl in range(4):
            h = heads[hl]
            P.op("gpsimd", lambda e, h=h, hl=hl: e.dma_start(
                out=Kq[0:64, hl * NCOLK: hl * NCOLK + 128 * ch1max],
                in_=t["d_Ks"][h, :, 0:128 * ch1max]), inc=("dK", 16))
            P.op("gpsimd", lambda e, h=h, hl=hl: e.dma_start(
                out=Kq[0:64, hl * NCOLK + 128 * ch1max: (hl + 1) * NCOLK],
                in_=t["d_Ks"][h, :, R2B: R2B + 128 * ch2max]), inc=("dK", 16))
        if hq == 0:
            P.op("vector", lambda e: e.memset(Kq[64:65, :], 1.0))
        for j in range(4):
            h = heads[j]
            P.op("gpsimd", lambda e, h=h, j=j: e.dma_start(
                out=r3o(Vq[:, j * nch * 65:], 0, 64, ch1max, 65),
                in_=t["d_Vs"][h, 0:128 * ch1max, :].rearrange("(c p) d -> p c d", p=128)),
                inc=("dV", 16))
            P.op("gpsimd", lambda e, h=h, j=j: e.dma_start(
                out=r3o(Vq[:, (j * nch + ch1max) * 65:], 0, 64, ch2max, 65),
                in_=t["d_Vs"][h, R2B: R2B + 128 * ch2max, :].rearrange("(c p) d -> p c d", p=128)),
                inc=("dV", 16))
        for hl in range(4):
            h = heads[hl]
            P.op("gpsimd", lambda e, h=h, hl=hl: e.dma_start(
                out=qTq[0:64, hl * nq:(hl + 1) * nq], in_=t["d_qT"][64 * h:64 * (h + 1), 0:nq]),
                inc=("dQ", 16))
        for p2 in range(2):
            pr = 2 * hq + p2
            P.op("gpsimd", lambda e, pr=pr, p2=p2: e.dma_start(
                out=vtailq[:, p2 * nq:(p2 + 1) * nq], in_=t["d_vtailT"][128 * pr:128 * (pr + 1), 0:nq]),
                inc=("dVT", 16))
        P.wait("vector", "dV")
        for j in range(4):
            P.op("vector", lambda e, j=j: e.tensor_copy(
                out=r3o(Vq[:, j * nch * 65:], 64, 1, ch1max, 65), in_=t["vones"][:, 0:ch1max]))
            P.op("vector", lambda e, j=j: e.tensor_copy(
                out=r3o(Vq[:, (j * nch + ch1max) * 65:], 64, 1, ch2max, 65),
                in_=t["vones"][:, 30:30 + ch2max]))
        P.wait("tensor", "S", t["attS"])
        nqc = (nq + 511) // 512
        for z in range(nqc):
            w = min(512, nq - 512 * z)
            if z > 0:
                P.wait("tensor", "S")
            P.op("tensor", lambda e, z=z, w=w, hq=hq: e.matmul(
                out=accp[0][0:4, 0:w],
                lhsT=t["eind4"].ap()[:, 4 * hq: 4 * hq + 4],
                rhs=t["att_tail"].ap()[:, 512 * z: 512 * z + w],
                start=True, stop=True), inc=("T", 1))
            tsg = P.cur("T")
            P.wait("scalar", "T", tsg)
            if z == 0:
                P.wait("scalar", "V")
            P.op("scalar", lambda e, z=z, w=w: e.copy(
                out=attq[0:4, 512 * z: 512 * z + w], in_=accp[0][0:4, 0:w]))
        st_accS[0] = P.cur("S")

        if KSTOPA == 1:
            continue
        P.wait("tensor", "dK")
        P.wait("tensor", "dQ")
        for iG, (G, t0, nqt) in enumerate(groups()):
            nmini = 7 * (G + 1)
            P.wait("gpsimd", "V")
            P.op("gpsimd", lambda e, G=G: e.dma_start(
                out=mmb[:, :], in_=t["mmasks"][G]), inc=("dG", 16))
            mmld = P.cur("dG")
            for hl in range(4):
                sl = hl % 2
                bank = slotp[sl]
                P.wait("tensor", "S")
                P.wait("tensor", "V")
                for ti in range(nqt):
                    qcol = (t0 + ti) * 128 - Q0
                    P.op("tensor", lambda e, hl=hl, ti=ti, qcol=qcol, G=G, nmini=nmini, bank=bank: e.matmul(
                        out=bank[:, ti * 128: ti * 128 + nmini],
                        lhsT=qTq[0:64, hl * nq + qcol: hl * nq + qcol + 128],
                        rhs=ap_view(Kq, hl * NCOLK + 128 * ch1max + 12,
                                    [[19, G + 1], [1, 7]], p0=0, np_=64),
                        start=True, stop=True), inc=("T", 1))
                tsg = P.cur("T")
                P.wait("vector", "T")
                P.wait("vector", "dG", mmld)
                P.op("vector", lambda e, nqt=nqt, G=G, t0=t0, bank=bank: e.tensor_tensor(
                    out=ap_view(bank, 7 * G, [[128, nqt], [1, 7]]),
                    in0=ap_view(bank, 7 * G, [[128, nqt], [1, 7]]),
                    in1=ap_view(mmb, (t0 - 4 * G) * 7, [[7, nqt], [1, 7]]),
                    op=ALU.mult))
                P.op("vector", lambda e, hl=hl, nqt=nqt, nmini=nmini, bank=bank: e.tensor_reduce(
                    out=ccols[:, 0:nqt],
                    in_=ap_view(bank, 0, [[128, nqt], [1, nmini]]),
                    axis=AX.X, op=ALU.max))
                P.op("vector", lambda e, nqt=nqt: e.tensor_scalar(
                    out=ccols[:, 8:8 + nqt], in0=ccols[:, 0:nqt],
                    scalar1=-1.0, scalar2=80.0, op0=ALU.mult, op1=ALU.add))
                vsg = P.cur("V")
                P.wait("tensor", "V", vsg)
                P.wait("tensor", "S", st_accS[0])
                P.op("tensor", lambda e, nqt=nqt: e.matmul(
                    out=accp[0][0:nqt, 0:128], lhsT=ccols[:, 8:8 + nqt],
                    rhs=t["ident"].ap()[:, :], start=True, stop=True), inc=("T", 1))
                tsg = P.cur("T")
                P.wait("scalar", "T", tsg)
                P.wait("scalar", "dC")
                P.op("scalar", lambda e, nqt=nqt: e.copy(
                    out=cbias[0:nqt, 0:128], in_=accp[0][0:nqt, 0:128]))
                st_accS[0] = P.cur("S")
                crow_S = P.cur("S")
                P.wait("gpsimd", "S", crow_S)
                qcol0 = t0 * 128 - Q0
                P.op("gpsimd", lambda e, hl=hl, nqt=nqt, qcol0=qcol0: e.dma_start(
                    out=qTq[64:65, hl * nq + qcol0: hl * nq + qcol0 + nqt * 128],
                    in_=cbias[0:nqt, 0:128]), inc=("dC", 16), dma=True)
                P.op("gpsimd", lambda e, hl=hl, nqt=nqt, qcol0=qcol0: e.dma_start(
                    out=cq4[hl:hl + 1, qcol0: qcol0 + nqt * 128],
                    in_=cbias[0:nqt, 0:128]), inc=("dC", 16), dma=True)
        P.wait("tensor", "dC")
        P.wait("scalar", "dC")
        P.wait("vector", "dC")
        P.wait("vector", "dVT")
        P.wait("vector", "dQ")
        if KSTOPA == 2:
            continue

        for iG, (G, t0, nqt) in enumerate(groups()):
            NQ = nqt * 128
            qoff = t0 * 128 - Q0
            moff = t0 * 128 - 512 * G
            chl = lchunks(G)
            P.wait("gpsimd", "V")
            for sm in range(5):
                P.op("gpsimd", lambda e, G=G, sm=sm: e.dma_start(
                    out=maskb[:, sm * 512:(sm + 1) * 512], in_=t["masks"][G, sm]), inc=("dG", 16))
            mload = P.cur("dG")
            P.wait("vector", "S")
            P.op("vector", lambda e, qoff=qoff, NQ=NQ: e.tensor_tensor(
                out=aexpq[0:4, qoff:qoff + NQ], in0=attq[0:4, qoff:qoff + NQ],
                in1=cq4[0:4, qoff:qoff + NQ], op=ALU.add))
            P.wait("scalar", "V")
            P.op("scalar", lambda e, qoff=qoff, NQ=NQ: e.activation(
                out=aexpq[0:4, qoff:qoff + NQ], in_=aexpq[0:4, qoff:qoff + NQ],
                func=AF.Exp, scale=0.125))
            aexp_S = P.cur("S")
            P.wait("tensor", "S")
            P.wait("tensor", "V")
            for ti in range(nqt):
                P.op("tensor", lambda e, ti=ti, qoff=qoff: e.matmul(
                    out=slotp[2][0:128, ti * 4: ti * 4 + 4],
                    lhsT=aexpq[0:4, qoff + 128 * ti: qoff + 128 * (ti + 1)],
                    rhs=t["identf32"].ap()[0:4, 0:4], start=True, stop=True),
                    inc=("T", 1) if ti == nqt - 1 else None)
            tsg = P.cur("T")
            P.wait("scalar", "T", tsg)
            P.wait("scalar", "V")
            P.op("scalar", lambda e, nqt=nqt: e.copy(out=atxs[:, 0:4 * nqt], in_=slotp[2][:, 0:4 * nqt]))
            st_slotS[2] = P.cur("S")
            if KDEBUG:
                P.wait("vector", "S")
                P.op("vector", lambda e, nqt=nqt, t0=t0: e.tensor_copy(
                    out=t["atall_dbg"][:, (t0 - Q0 // 128) * 4:(t0 - Q0 // 128) * 4 + 4 * nqt],
                    in_=atxs[:, 0:4 * nqt]))

            mask_map = {}
            for k, gchunk in enumerate(MASK_META[G]):
                lc = gchunk if gchunk < 30 else ch1max + (gchunk - 30)
                mask_map[lc] = k

            for p2 in range(2):
                nb = (len(chl) + 1) // 2
                for h2 in range(2):
                    pass
                first = {0: True, 1: True}
                for cb in range(nb):
                    cs = chl[2 * cb: 2 * cb + 2]
                    for h2 in range(2):
                        hl = 2 * p2 + h2
                        sl = rot[0] % 3
                        rot[0] += 1
                        P.wait("tensor", "S")
                        P.wait("tensor", "V")
                        hl_ = 2 * p2 + h2
                        for j, c in enumerate(cs):
                            P.op("tensor", lambda e, hl_=hl_, c=c, j=j, sl=sl, qoff=qoff, NQ=NQ: e.matmul(
                                out=slotp[sl][:, j * 512: j * 512 + NQ],
                                lhsT=Kq[0:65, hl_ * NCOLK + 128 * c: hl_ * NCOLK + 128 * (c + 1)],
                                rhs=qTq[0:65, hl_ * nq + qoff: hl_ * nq + qoff + NQ],
                                start=True, stop=True), inc=("T", 1))
                        tsg = P.cur("T")
                        nvis = 0
                        for j, c in enumerate(cs):
                            if c in mask_map:
                                k = mask_map[c]
                                P.wait("vector", "T", tsg)
                                P.wait("vector", "dG", mload)
                                P.op("vector", lambda e, sl=sl, j=j, k=k, NQ=NQ, moff=moff: e.tensor_tensor(
                                    out=slotp[sl][:, j * 512: j * 512 + NQ],
                                    in0=slotp[sl][:, j * 512: j * 512 + NQ],
                                    in1=maskb[:, k * 512 + moff: k * 512 + moff + NQ], op=ALU.add))
                                nvis += 1
                        mask_V = P.cur("V")
                        P.wait("scalar", "T", tsg)
                        if nvis:
                            P.wait("scalar", "V", mask_V)
                        P.wait("scalar", "T", st_AbufT[sl])
                        P.op("scalar", lambda e, sl=sl, NQ=NQ, n=len(cs): e.activation(
                            out=ap_view(Abuf, sl * 1024, [[512, n], [1, NQ]]),
                            in_=ap_view(slotp[sl], 0, [[512, n], [1, NQ]]),
                            func=AF.Exp, scale=0.125))
                        st_slotS[sl] = P.cur("S")
                        exp_S = P.cur("S")
                        if KSTOPA == 3:
                            continue
                        P.wait("tensor", "S", exp_S)
                        if first[h2]:
                            P.wait("tensor", "S")
                        for j, c in enumerate(cs):
                            P.op("tensor", lambda e, hl=hl, h2=h2, c=c, j=j, sl=sl, NQ=NQ, st=first[h2] and j == 0: e.matmul(
                                out=accp[h2][0:65, 0:NQ],
                                lhsT=Vq[:, (hl * nch + c) * 65: (hl * nch + c) * 65 + 65],
                                rhs=Abuf[:, sl * 1024 + j * 512: sl * 1024 + j * 512 + NQ],
                                start=st, stop=False, skip_group_check=True), inc=("T", 1))
                        first[h2] = False
                        st_AbufT[sl] = P.cur("T")
                av_T = P.cur("T")
                if KSTOPA in (3, 4):
                    continue
                P.wait("scalar", "T", av_T)
                for h2 in range(2):
                    P.op("scalar", lambda e, h2=h2, NQ=NQ: e.copy(
                        out=accS[0:65, h2 * 544: h2 * 544 + NQ], in_=accp[h2][0:65, 0:NQ]))
                    st_accS[h2] = P.cur("S")
                ev_S = P.cur("S")
                if KSTOPA == 5:
                    continue
                slv = rot[0] % 3
                rot[0] += 1
                sla = rot[0] % 3
                rot[0] += 1
                P.wait("tensor", "S")
                P.wait("tensor", "V")
                for ti in range(nqt):
                    for h2 in range(2):
                        P.op("tensor", lambda e, ti=ti, h2=h2, p2=p2, qoff=qoff, slv=slv: e.matmul(
                            out=slotp[slv][:, h2 * 512 + ti * 64: h2 * 512 + ti * 64 + 64],
                            lhsT=vtailq[64 * h2:64 * (h2 + 1), p2 * nq + qoff + 128 * ti: p2 * nq + qoff + 128 * (ti + 1)],
                            rhs=t["ident"].ap()[64 * h2:64 * (h2 + 1), 64 * h2:64 * (h2 + 1)],
                            start=True, stop=True), inc=("T", 1))
                        P.op("tensor", lambda e, ti=ti, h2=h2, qoff=qoff, sla=sla: e.matmul(
                            out=slotp[sla][:, h2 * 512 + ti * 65: h2 * 512 + ti * 65 + 65],
                            lhsT=accS[0:65, h2 * 544 + 128 * ti: h2 * 544 + 128 * (ti + 1)],
                            rhs=t["ident"].ap()[0:65, 0:65], start=True, stop=True), inc=("T", 1))
                tp_T = P.cur("T")
                if KSTOPA == 6:
                    continue
                P.wait("scalar", "T", tp_T)
                P.wait("scalar", "V")
                P.op("scalar", lambda e, nqt=nqt, slv=slv: e.copy(
                    out=ap_view(vtts, 0, [[256, 2], [64, nqt], [1, 64]]),
                    in_=ap_view(slotp[slv], 0, [[512, 2], [64, nqt], [1, 64]])))
                if KDEBUG and p2 == 0 and iG < 5:
                    P.wait("vector", "S")
                    P.op("vector", lambda e, iG=iG: e.tensor_copy(
                        out=t["vtall_dbg"][:, iG * 512:(iG + 1) * 512], in_=vtts[:, 0:512]))
                st_slotS[slv] = P.cur("S")
                vt_S = P.cur("S")
                P.wait("vector", "T", tp_T)
                P.wait("vector", "S", vt_S)
                P.op("vector", lambda e, nqt=nqt, p2=p2, sla=sla: e.tensor_tensor(
                    out=ap_view(zs, 0, [[2, nqt], [1, 2]]),
                    in0=ap_view(slotp[sla], 64, [[65, nqt], [512, 2]]),
                    in1=ap_view(atxs, 2 * p2, [[4, nqt], [1, 2]]), op=ALU.add))
                P.op("vector", lambda e, nqt=nqt: e.tensor_scalar_add(
                    out=zs[:, 0:2 * nqt], in0=zs[:, 0:2 * nqt], scalar1=1e-30))
                if KDEBUG:
                    P.op("vector", lambda e, nqt=nqt, t0=t0, p2=p2: e.tensor_copy(
                        out=ap_view(t["zall_dbg"], (t0 - Q0 // 128) * 4 + 2 * p2, [[4, nqt], [1, 2]]),
                        in_=ap_view(zs, 0, [[2, nqt], [1, 2]])))
                P.op("vector", lambda e, nqt=nqt: e.reciprocal(out=rzs[:, 0:2 * nqt], in_=zs[:, 0:2 * nqt]))
                if KSTOPA == 7:
                    continue
                P.op("vector", lambda e, nqt=nqt, p2=p2: e.tensor_tensor(
                    out=ap_view(vtts, 0, [[256, 2], [64, nqt], [1, 64]]),
                    in0=ap_view(vtts, 0, [[256, 2], [64, nqt], [1, 64]]),
                    in1=ap_view(atxs, 2 * p2, [[1, 2], [4, nqt], [0, 64]]), op=ALU.mult))
                if KSTOPA == 8:
                    continue
                P.op("vector", lambda e, nqt=nqt, sla=sla: e.tensor_tensor(
                    out=ap_view(vtts, 0, [[256, 2], [64, nqt], [1, 64]]),
                    in0=ap_view(vtts, 0, [[256, 2], [64, nqt], [1, 64]]),
                    in1=ap_view(slotp[sla], 0, [[512, 2], [65, nqt], [1, 64]]), op=ALU.add))
                if KSTOPA == 9:
                    continue
                tile0_l = t0 - Q0 // 128
                P.op("vector", lambda e, nqt=nqt, p2=p2, tile0_l=tile0_l, hq=hq: e.tensor_tensor(
                    out=ap_view(attn_out, tile0_l * 1024 + 256 * hq + 128 * p2, [[64, 2], [1024, nqt], [1, 64]]),
                    in0=ap_view(vtts, 0, [[256, 2], [64, nqt], [1, 64]]),
                    in1=ap_view(rzs, 0, [[1, 2], [2, nqt], [0, 64]]), op=ALU.mult))
                st_slotV[sla] = P.cur("V")
                st_slotV[slv] = P.cur("V")
    P.wait("gpsimd", "V")
    for ti in range(ntq):
        P.op("gpsimd", lambda e, ti=ti: e.dma_start(
            out=t["d_ao"][ti * 128:(ti + 1) * 128, :],
            in_=attn_out[:, ti * 1024:(ti + 1) * 1024]), inc=("dQ", 16))
    P.op("gpsimd", lambda e: e.engine_nop())
    if not KDEBUG:
        ar.release()


def emit_ffn(P, cfg, t):
    P.barrier()
    nq, Q0, T0, T1 = cfg["nq"], cfg["Q0"], cfg["T0"], cfg["T1"]
    ntq = nq // 128
    T0c, T1c = T0 - Q0, T1 - Q0
    ar = t["ar"]
    pw0, pw1 = t["pw0"], t["pw1"]
    atail = t["att_tail"].ap()

    ar.mark()
    scal = ar.alloc(F32, 8)
    onesc = ar.alloc(F32, 1)
    onesp = ar.alloc(F32, 128, parts=1)
    epsb = ar.alloc(F32, 1)
    P.op("vector", lambda e: e.memset(onesc, 1.0))
    P.op("vector", lambda e: e.memset(onesp, 1.0))
    P.op("vector", lambda e: e.memset(epsb, 1e-5))

    ar.mark()
    wpb = ar.alloc(BF16, 8 * 1024)
    aob = ar.alloc(BF16, 2 * 1024)
    xrb = ar.alloc(F32, 2 * 1024)
    aoTs = ar.alloc(BF16, 1024)
    x1b = ar.alloc(F32, 1024)
    zf = ar.alloc(F32, 1024)
    rb = ar.alloc(BF16, 1024)
    abf = ar.alloc(BF16, 1024)
    bbf = ar.alloc(BF16, 1024)
    xqtb = ar.alloc(BF16, 1024)
    st = ar.alloc(F32, 16)
    for c in range(8):
        P.op("gpsimd", lambda e, c=c: e.dma_start(
            out=wpb[:, c * 1024:(c + 1) * 1024],
            in_=t["wprojT"][128 * c:128 * (c + 1), :]), inc=("dW", 16))
    for ti in range(ntq):
        pr = ti % 2
        P.wait("gpsimd", "T")
        P.wait("gpsimd", "V")
        P.op("gpsimd", lambda e, ti=ti, pr=pr: e.dma_start(
            out=aob[:, pr * 1024:(pr + 1) * 1024],
            in_=t["d_ao"][ti * 128:(ti + 1) * 128, :]), inc=("dK", 16))
        P.op("gpsimd", lambda e, ti=ti, pr=pr: e.dma_start(
            out=xrb[:, pr * 1024:(pr + 1) * 1024],
            in_=t["xres"][ti * 128:(ti + 1) * 128, :]), inc=("dV", 16))
        dk = P.cur("dK")
        P.wait("tensor", "dK", dk)
        if ti == 0:
            P.wait("tensor", "dW")
        P.wait("tensor", "S")
        P.wait("tensor", "V")
        for cb in range(8):
            P.op("tensor", lambda e, cb=cb, pr=pr: e.matmul(
                out=pw0[:, cb * 128:(cb + 1) * 128],
                lhsT=aob[:, pr * 1024 + cb * 128: pr * 1024 + (cb + 1) * 128],
                rhs=t["ident"].ap()[:, :], start=True, stop=True), inc=("T", 1))
        tT = P.cur("T")
        P.wait("scalar", "T", tT)
        P.op("scalar", lambda e: e.copy(out=aoTs[:, :], in_=pw0[:, 0:1024]))
        sS = P.cur("S")
        P.wait("tensor", "S", sS)
        for half in range(2):
            for cb in range(8):
                P.op("tensor", lambda e, half=half, cb=cb: e.matmul(
                    out=pw1[:, half * 512:(half + 1) * 512],
                    lhsT=aoTs[:, cb * 128:(cb + 1) * 128],
                    rhs=wpb[:, cb * 1024 + half * 512: cb * 1024 + half * 512 + 512],
                    start=(cb == 0), stop=(cb == 7)), inc=("T", 1))
        tT = P.cur("T")
        P.wait("vector", "T", tT)
        P.wait("vector", "dV")
        P.wait("vector", "dX1")
        P.wait("vector", "dZ")
        P.op("vector", lambda e, pr=pr: e.tensor_tensor(
            out=x1b[:, :], in0=pw1[:, 0:1024], in1=xrb[:, pr * 1024:(pr + 1) * 1024],
            op=ALU.add))
        P.op("vector", lambda e: e.tensor_reduce(
            out=st[:, 0:1], in_=x1b[:, :], axis=AX.X, op=ALU.add))
        vV = P.cur("V")
        P.wait("scalar", "V", vV)
        P.op("scalar", lambda e: e.activation(
            out=zf[:, :], in_=x1b[:, :], func=AF.Square, accum_out=st[:, 1:2]))
        sS = P.cur("S")
        P.wait("vector", "S", sS)
        P.op("vector", lambda e: e.tensor_scalar_mul(
            out=st[:, 2:3], in0=st[:, 0:1], scalar1=1.0 / 1024.0))
        P.op("vector", lambda e: e.tensor_scalar_mul(
            out=st[:, 3:4], in0=st[:, 1:2], scalar1=1.0 / 1024.0))
        P.op("vector", lambda e: e.tensor_tensor(
            out=st[:, 4:5], in0=st[:, 2:3], in1=st[:, 2:3], op=ALU.mult))
        P.op("vector", lambda e: e.tensor_tensor(
            out=st[:, 4:5], in0=st[:, 3:4], in1=st[:, 4:5], op=ALU.subtract))
        P.op("vector", lambda e: e.tensor_scalar_add(
            out=st[:, 4:5], in0=st[:, 4:5], scalar1=1e-5))
        vV = P.cur("V")
        P.wait("scalar", "V", vV)
        P.op("scalar", lambda e: e.activation(
            out=st[:, 5:6], in_=st[:, 4:5], func=AF.Sqrt))
        sS = P.cur("S")
        P.wait("vector", "S", sS)
        P.op("vector", lambda e: e.reciprocal(out=st[:, 5:6], in_=st[:, 5:6]))
        P.op("vector", lambda e: e.tensor_tensor(
            out=st[:, 6:7], in0=st[:, 2:3], in1=st[:, 5:6], op=ALU.mult))
        P.op("vector", lambda e: e.tensor_scalar_mul(
            out=st[:, 6:7], in0=st[:, 6:7], scalar1=-1.0))
        P.op("vector", lambda e: e.tensor_scalar(
            out=x1b[:, :], in0=x1b[:, :], scalar1=st[:, 5:6], scalar2=st[:, 6:7],
            op0=ALU.mult, op1=ALU.add))
        vV = P.cur("V")
        P.wait("gpsimd", "V", vV)
        P.op("gpsimd", lambda e, ti=ti: e.dma_start(
            out=t["d_r1"][ti * 128:(ti + 1) * 128, :], in_=x1b[:, :]), inc=("dX1", 16))
        P.op("vector", lambda e: e.tensor_reduce(
            out=st[:, 7:8], in_=x1b[:, :], axis=AX.X, op=ALU.max))
        P.op("vector", lambda e: e.tensor_reduce(
            out=st[:, 8:9], in_=x1b[:, :], axis=AX.X, op=ALU.min))
        P.op("vector", lambda e: e.tensor_scalar_mul(
            out=st[:, 8:9], in0=st[:, 8:9], scalar1=-1.0))
        P.op("vector", lambda e: e.tensor_tensor(
            out=st[:, 7:8], in0=st[:, 7:8], in1=st[:, 8:9], op=ALU.max))
        P.op("vector", lambda e: e.tensor_scalar_max(
            out=st[:, 7:8], in0=st[:, 7:8], scalar1=1e-5))
        P.op("vector", lambda e: e.reciprocal(out=st[:, 9:10], in_=st[:, 7:8]))
        vV = P.cur("V")
        P.wait("gpsimd", "V", vV)
        P.op("gpsimd", lambda e, ti=ti: e.dma_start(
            out=atail[0:1, ti * 128:(ti + 1) * 128], in_=st[:, 7:8]), inc=("dZ", 16))
        P.op("vector", lambda e: e.tensor_scalar_mul(
            out=st[:, 9:10], in0=st[:, 9:10], scalar1=127.0))
        P.op("vector", lambda e: e.tensor_scalar_mul(
            out=zf[:, :], in0=x1b[:, :], scalar1=st[:, 9:10]))
        P.op("vector", lambda e: e.tensor_scalar(
            out=abf[:, :], in0=zf[:, :], scalar1=0.0, scalar2=128.0, op0=ALU.max, op1=ALU.add))
        P.op("vector", lambda e: e.tensor_scalar(
            out=bbf[:, :], in0=zf[:, :], scalar1=0.0, scalar2=-128.0, op0=ALU.min, op1=ALU.add))
        P.op("vector", lambda e: e.tensor_tensor(
            out=rb[:, :], in0=abf[:, :], in1=bbf[:, :], op=ALU.add))
        vV = P.cur("V")
        P.wait("tensor", "V", vV)
        P.wait("tensor", "S")
        for cb in range(8):
            P.op("tensor", lambda e, cb=cb: e.matmul(
                out=pw1[:, 1024 + cb * 128: 1024 + (cb + 1) * 128],
                lhsT=rb[:, cb * 128:(cb + 1) * 128],
                rhs=t["ident"].ap()[:, :], start=True, stop=True), inc=("T", 1))
        tT = P.cur("T")
        P.wait("scalar", "T", tT)
        P.wait("scalar", "dQ")
        P.op("scalar", lambda e: e.copy(out=xqtb[:, :], in_=pw1[:, 1024:2048]))
        sS = P.cur("S")
        P.wait("gpsimd", "S", sS)
        for cb in range(8):
            P.op("gpsimd", lambda e, cb=cb, ti=ti: e.dma_start(
                out=t["d_xq"][cb * 128:(cb + 1) * 128, ti * 128:(ti + 1) * 128],
                in_=xqtb[:, cb * 128:(cb + 1) * 128]), inc=("dQ", 16))
    ar.release()
    P.barrier()

    ar.mark()
    wb16 = ar.alloc(BF16, 2 * 4096)
    wdmp = ar.alloc(BF16, 4096)
    macc = ar.alloc(F32, 4)
    wf32 = ar.alloc(F32, 2 * 4096)
    zf0 = ar.alloc(F32, 4096)
    tA = ar.alloc(BF16, 4096)
    tB = ar.alloc(BF16, 4096)
    tT0 = ar.alloc(BF16, 4096)
    P.op("vector", lambda e: e.memset(macc[:, 0:2], 0.0))

    def absmean_pass(src_ap, nchunks, ncols, mcol):
        for c in range(nchunks):
            pr = c % 2
            P.wait("gpsimd", "S")
            P.op("gpsimd", lambda e, c=c, pr=pr: e.dma_start(
                out=wb16[:, pr * 4096: pr * 4096 + ncols],
                in_=src_ap[128 * c:128 * (c + 1), :]), inc=("dW", 16))
            P.wait("scalar", "dW")
            P.wait("scalar", "V")
            P.op("scalar", lambda e, pr=pr, ncols=ncols: e.activation(
                out=wdmp[:, 0:ncols], in_=wb16[:, pr * 4096: pr * 4096 + ncols],
                func=AF.Abs, accum_out=macc[:, 2:3]))
            sS = P.cur("S")
            P.wait("vector", "S", sS)
            P.op("vector", lambda e, mcol=mcol: e.tensor_tensor(
                out=macc[:, mcol:mcol + 1], in0=macc[:, mcol:mcol + 1],
                in1=macc[:, 2:3], op=ALU.add))

    absmean_pass(t["w1T"], 8, 4096, 0)
    absmean_pass(t["w2T"], 32, 1024, 1)
    P.wait("tensor", "V")
    P.wait("tensor", "S")
    for i in range(2):
        P.op("tensor", lambda e, i=i: e.matmul(
            out=pw0[0:1, i:i + 1], lhsT=macc[:, i:i + 1], rhs=onesc[:, :],
            start=True, stop=True), inc=("T", 1))
    tT = P.cur("T")
    P.wait("scalar", "T", tT)
    P.op("scalar", lambda e: e.copy(out=macc[0:1, 2:4], in_=pw0[0:1, 0:2]))
    sS = P.cur("S")
    P.wait("vector", "S", sS)
    P.op("vector", lambda e: e.reciprocal(out=macc[0:1, 0:2], in_=macc[0:1, 2:4]))
    P.op("vector", lambda e: e.tensor_scalar_mul(
        out=macc[0:1, 0:2], in0=macc[0:1, 0:2], scalar1=4096.0 * 1024.0))
    P.op("vector", lambda e: e.tensor_scalar_mul(
        out=macc[0:1, 2:4], in0=macc[0:1, 2:4], scalar1=1.0 / (4096.0 * 1024.0 * 127.0)))
    vV = P.cur("V")
    P.wait("tensor", "V", vV)
    for i, src in enumerate((0, 2, 1, 3)):
        P.op("tensor", lambda e, i=i, src=src: e.matmul(
            out=pw0[:, 128 + i:128 + i + 1], lhsT=onesp[:, :],
            rhs=macc[0:1, src:src + 1], start=True, stop=True), inc=("T", 1))
    tT = P.cur("T")
    P.wait("scalar", "T", tT)
    P.op("scalar", lambda e: e.copy(out=scal[:, 0:4], in_=pw0[:, 128:132]))
    P.wait("gpsimd", "S")

    def tern_pass(src_ap, dst_ap, nchunks, ncols, rmcol):
        for c in range(nchunks):
            pr = c % 2
            P.wait("gpsimd", "V")
            P.op("gpsimd", lambda e, c=c, pr=pr: e.dma_start(
                out=wf32[:, pr * 4096: pr * 4096 + ncols],
                in_=src_ap[128 * c:128 * (c + 1), :]), inc=("dW", 16))
            P.wait("vector", "dW")
            P.wait("vector", "S")
            P.wait("vector", "dS")
            P.op("vector", lambda e, pr=pr, ncols=ncols, rmcol=rmcol: e.tensor_scalar_mul(
                out=zf0[:, 0:ncols], in0=wf32[:, pr * 4096: pr * 4096 + ncols],
                scalar1=scal[:, rmcol:rmcol + 1]))
            P.op("vector", lambda e, ncols=ncols: e.tensor_scalar(
                out=tA[:, 0:ncols], in0=zf0[:, 0:ncols], scalar1=0.0, scalar2=128.0,
                op0=ALU.max, op1=ALU.add))
            P.op("vector", lambda e, ncols=ncols: e.tensor_scalar(
                out=tB[:, 0:ncols], in0=zf0[:, 0:ncols], scalar1=0.0, scalar2=-128.0,
                op0=ALU.min, op1=ALU.add))
            P.op("vector", lambda e, ncols=ncols: e.tensor_tensor(
                out=tT0[:, 0:ncols], in0=tA[:, 0:ncols], in1=tB[:, 0:ncols], op=ALU.add))
            P.op("vector", lambda e, ncols=ncols: e.tensor_scalar(
                out=tT0[:, 0:ncols], in0=tT0[:, 0:ncols], scalar1=1.0, scalar2=-1.0,
                op0=ALU.min, op1=ALU.max))
            vV = P.cur("V")
            P.wait("gpsimd", "V", vV)
            P.op("gpsimd", lambda e, c=c, ncols=ncols: e.dma_start(
                out=dst_ap[128 * c:128 * (c + 1), :], in_=tT0[:, 0:ncols]), inc=("dS", 16))

    tern_pass(t["w1T"], t["d_w1t"], 8, 4096, 0)
    tern_pass(t["w2T"], t["d_w2t"], 32, 1024, 2)
    ar.release()
    P.barrier()

    NS0 = 1024
    NT = T1c - T0c
    nsl = (NT + NS0 - 1) // NS0
    bounds = [T0c + min(i * NS0, NT) for i in range(nsl + 1)]

    ar.mark()
    NSmax = NS0
    xq1s = ar.alloc(BF16, 8 * (NSmax + 2))
    hsl = [ar.alloc(BF16, NSmax + 2) for _ in range(32)]
    cvb = ar.alloc(BF16, NSmax)
    mxb = ar.alloc(BF16, NSmax)
    mnb = ar.alloc(BF16, NSmax)
    zsf = ar.alloc(F32, NSmax)
    hqa = ar.alloc(BF16, NSmax)
    hqb = ar.alloc(BF16, NSmax)
    a1sl = ar.alloc(F32, NSmax + 2)
    s2sl = ar.alloc(F32, NSmax)
    b2sl = ar.alloc(F32, NSmax)
    yfm = ar.alloc(BF16, 8 * NSmax)
    rows3 = ar.alloc(F32, 2 * NSmax)
    w1tb = ar.alloc(BF16, 2 * 1024)
    w2tb = ar.alloc(BF16, 2 * 128)
    x1rl = ar.alloc(BF16, 2 * 1024)
    x2b = ar.alloc(F32, 1024)
    outb = ar.alloc(F32, 2 * 1024)
    st2 = ar.alloc(F32, 16)
    cwb = ar.alloc(F32, 96)
    cbb = ar.alloc(F32, 32)
    P.op("gpsimd", lambda e: e.dma_start(out=cwb[:, :], in_=t["convw"]), inc=("dW", 16))
    P.op("gpsimd", lambda e: e.dma_start(out=cbb[:, :], in_=t["convb"]), inc=("dW", 16))
    P.wait("gpsimd", "dX1")
    P.wait("gpsimd", "dQ")
    P.wait("gpsimd", "dS")
    P.wait("scalar", "dW")
    P.wait("vector", "dW")

    for isl in range(nsl):
        s0, s1 = bounds[isl], bounds[isl + 1]
        NS = s1 - s0
        nb = NS // 128
        k0 = 1 if s0 - 1 < 0 else 0
        k1 = (min(nq, s1 + 1)) - (s0 - 1)
        P.wait("gpsimd", "T")
        for ci in range(8):
            P.op("gpsimd", lambda e, ci=ci, s0=s0, k0=k0, k1=k1, NS=NS: e.dma_start(
                out=xq1s[:, ci * (NS + 2) + k0: ci * (NS + 2) + k1],
                in_=t["d_xq"][128 * ci:128 * (ci + 1), s0 - 1 + k0: s0 - 1 + k1]),
                inc=("dQ", 16))
        P.wait("tensor", "S")
        P.wait("tensor", "V")
        for gb in range(0, k1, 512):
            g0, g1 = max(gb, k0), min(k1, gb + 512)
            if g0 >= g1:
                continue
            P.op("tensor", lambda e, g0=g0, g1=g1, s0=s0: e.matmul(
                out=pw1[:, g0:g1], lhsT=onesp[:, :],
                rhs=atail[0:1, s0 - 1 + g0: s0 - 1 + g1], start=True, stop=True),
                inc=("T", 1))
        tT = P.cur("T")
        P.wait("vector", "T", tT)
        P.op("vector", lambda e, k0=k0, k1=k1: e.tensor_scalar_mul(
            out=a1sl[:, k0:k1], in0=pw1[:, k0:k1], scalar1=scal[:, 1:2]))
        P.op("vector", lambda e, NS=NS: e.memset(mxb[:, 0:NS], -1e30))
        P.op("vector", lambda e, NS=NS: e.memset(mnb[:, 0:NS], 1e30))

        for c in range(32):
            prc = c % 2
            P.wait("gpsimd", "T")
            for ci in range(8):
                P.op("gpsimd", lambda e, ci=ci, c=c, prc=prc: e.dma_start(
                    out=w1tb[:, prc * 1024 + ci * 128: prc * 1024 + (ci + 1) * 128],
                    in_=t["d_w1t"][128 * ci:128 * (ci + 1), 128 * c:128 * (c + 1)]),
                    inc=("dK", 16))
            dk = P.cur("dK")
            P.wait("tensor", "dK", dk)
            P.wait("tensor", "dQ")
            P.wait("tensor", "S")
            P.wait("tensor", "V")
            for gb in range(0, k1, 512):
                g0, g1 = max(gb, k0), min(k1, gb + 512)
                if g0 >= g1:
                    continue
                for ci in range(8):
                    P.op("tensor", lambda e, ci=ci, g0=g0, g1=g1, prc=prc, NS=NS: e.matmul(
                        out=pw0[:, g0:g1],
                        lhsT=w1tb[:, prc * 1024 + ci * 128: prc * 1024 + (ci + 1) * 128],
                        rhs=xq1s[:, ci * (NS + 2) + g0: ci * (NS + 2) + g1],
                        start=(ci == 0), stop=(ci == 7)), inc=("T", 1))
            tT = P.cur("T")
            P.wait("vector", "T", tT)
            if k0 == 1:
                P.op("vector", lambda e, c=c: e.memset(hsl[c][:, 0:1], 0.0))
            if k1 < NS + 2:
                P.op("vector", lambda e, c=c, k1=k1, NS=NS: e.memset(hsl[c][:, k1:NS + 2], 0.0))
            P.op("vector", lambda e, c=c, k0=k0, k1=k1: e.tensor_tensor(
                out=hsl[c][:, k0:k1], in0=pw0[:, k0:k1], in1=a1sl[:, k0:k1], op=ALU.mult))
            P.op("vector", lambda e, c=c, NS=NS: e.tensor_scalar_mul(
                out=cvb[:, 0:NS], in0=hsl[c][:, 2:NS + 2],
                scalar1=cwb[:, 3 * c + 2:3 * c + 3]))
            P.op("vector", lambda e, c=c, NS=NS: e.scalar_tensor_tensor(
                out=cvb[:, 0:NS], in0=hsl[c][:, 0:NS],
                scalar=cwb[:, 3 * c:3 * c + 1], in1=cvb[:, 0:NS],
                op0=ALU.mult, op1=ALU.add))
            P.op("vector", lambda e, c=c, NS=NS: e.scalar_tensor_tensor(
                out=cvb[:, 0:NS], in0=hsl[c][:, 1:NS + 1],
                scalar=cwb[:, 3 * c + 1:3 * c + 2], in1=cvb[:, 0:NS],
                op0=ALU.mult, op1=ALU.add))
            vV = P.cur("V")
            P.wait("scalar", "V", vV)
            P.op("scalar", lambda e, c=c, NS=NS: e.activation(
                out=hsl[c][:, 0:NS], in_=cvb[:, 0:NS], func=AF.Silu,
                bias=cbb[:, c:c + 1]))
            sS = P.cur("S")
            P.wait("vector", "S", sS)
            P.op("vector", lambda e, c=c, NS=NS: e.tensor_tensor(
                out=mxb[:, 0:NS], in0=mxb[:, 0:NS], in1=hsl[c][:, 0:NS], op=ALU.max))
            P.op("vector", lambda e, c=c, NS=NS: e.tensor_tensor(
                out=mnb[:, 0:NS], in0=mnb[:, 0:NS], in1=hsl[c][:, 0:NS], op=ALU.min))
        P.op("vector", lambda e, NS=NS: e.tensor_scalar_mul(
            out=mnb[:, 0:NS], in0=mnb[:, 0:NS], scalar1=-1.0))
        P.op("vector", lambda e, NS=NS: e.tensor_tensor(
            out=mxb[:, 0:NS], in0=mxb[:, 0:NS], in1=mnb[:, 0:NS], op=ALU.max))
        vV = P.cur("V")
        P.wait("tensor", "V", vV)
        P.wait("tensor", "S")
        for b in range(nb):
            P.op("tensor", lambda e, b=b: e.matmul(
                out=pw1[:, 1024 + b * 128: 1024 + (b + 1) * 128],
                lhsT=mxb[:, b * 128:(b + 1) * 128], rhs=t["ident"].ap()[:, :],
                start=True, stop=True), inc=("T", 1))
        tT = P.cur("T")
        P.wait("vector", "T", tT)
        P.wait("vector", "dZ")
        for b in range(nb):
            P.op("vector", lambda e, b=b: e.tensor_reduce(
                out=st2[:, b:b + 1], in_=pw1[:, 1024 + b * 128: 1024 + (b + 1) * 128],
                axis=AX.X, op=ALU.max))
        P.op("vector", lambda e, nb=nb: e.tensor_scalar_max(
            out=st2[:, 0:nb], in0=st2[:, 0:nb], scalar1=1e-5))
        P.op("vector", lambda e, nb=nb: e.reciprocal(
            out=st2[:, 8:8 + nb], in_=st2[:, 0:nb]))
        P.op("vector", lambda e, nb=nb: e.tensor_scalar_mul(
            out=st2[:, 8:8 + nb], in0=st2[:, 8:8 + nb], scalar1=127.0))
        P.op("vector", lambda e, nb=nb: e.tensor_scalar_mul(
            out=st2[:, 0:nb], in0=st2[:, 0:nb], scalar1=scal[:, 3:4]))
        vV = P.cur("V")
        P.wait("gpsimd", "V", vV)
        for b in range(nb):
            P.op("gpsimd", lambda e, b=b: e.dma_start(
                out=rows3[0:1, b * 128:(b + 1) * 128],
                in_=st2[:, 8 + b:8 + b + 1]), inc=("dZ", 16))
            P.op("gpsimd", lambda e, b=b, NSmax=NSmax: e.dma_start(
                out=rows3[0:1, NSmax + b * 128: NSmax + (b + 1) * 128],
                in_=st2[:, b:b + 1]), inc=("dZ", 16))
        P.wait("tensor", "dZ")
        P.wait("tensor", "V")
        P.wait("tensor", "S")
        for g0 in range(0, NS, 512):
            g1 = min(NS, g0 + 512)
            P.op("tensor", lambda e, g0=g0, g1=g1: e.matmul(
                out=pw1[:, g0:g1], lhsT=onesp[:, :],
                rhs=rows3[0:1, g0:g1], start=True, stop=True), inc=("T", 1))
            P.op("tensor", lambda e, g0=g0, g1=g1, NSmax=NSmax: e.matmul(
                out=pw1[:, 1024 + g0: 1024 + g1], lhsT=onesp[:, :],
                rhs=rows3[0:1, NSmax + g0: NSmax + g1], start=True, stop=True), inc=("T", 1))
        tT = P.cur("T")
        P.wait("vector", "T", tT)
        P.op("vector", lambda e, NS=NS: e.tensor_copy(out=s2sl[:, 0:NS], in_=pw1[:, 0:NS]))
        P.op("vector", lambda e, NS=NS: e.tensor_copy(out=b2sl[:, 0:NS], in_=pw1[:, 1024:1024 + NS]))
        for c in range(32):
            P.op("vector", lambda e, c=c, NS=NS: e.tensor_tensor(
                out=zsf[:, 0:NS], in0=hsl[c][:, 0:NS], in1=s2sl[:, 0:NS], op=ALU.mult))
            P.op("vector", lambda e, NS=NS: e.tensor_scalar(
                out=hqa[:, 0:NS], in0=zsf[:, 0:NS], scalar1=0.0, scalar2=128.0,
                op0=ALU.max, op1=ALU.add))
            P.op("vector", lambda e, NS=NS: e.tensor_scalar(
                out=hqb[:, 0:NS], in0=zsf[:, 0:NS], scalar1=0.0, scalar2=-128.0,
                op0=ALU.min, op1=ALU.add))
            P.op("vector", lambda e, c=c, NS=NS: e.tensor_tensor(
                out=hsl[c][:, 0:NS], in0=hqa[:, 0:NS], in1=hqb[:, 0:NS], op=ALU.add))
            P.op("vector", lambda e, c=c, NS=NS: e.tensor_scalar(
                out=hsl[c][:, 0:NS], in0=hsl[c][:, 0:NS], scalar1=127.0, scalar2=-128.0,
                op0=ALU.min, op1=ALU.max))
        for cob in range(8):
            P.wait("tensor", "S")
            P.wait("tensor", "V")
            for c in range(32):
                prc = c % 2
                P.wait("gpsimd", "T")
                P.op("gpsimd", lambda e, c=c, cob=cob, prc=prc: e.dma_start(
                    out=w2tb[:, prc * 128:(prc + 1) * 128],
                    in_=t["d_w2t"][128 * c:128 * (c + 1), 128 * cob:128 * (cob + 1)]),
                    inc=("dV", 16))
                dv = P.cur("dV")
                P.wait("tensor", "dV", dv)
                for g0 in range(0, NS, 512):
                    g1 = min(NS, g0 + 512)
                    P.op("tensor", lambda e, c=c, g0=g0, g1=g1, prc=prc: e.matmul(
                        out=pw1[:, g0:g1],
                        lhsT=w2tb[:, prc * 128:(prc + 1) * 128],
                        rhs=hsl[c][:, g0:g1],
                        start=(c == 0), stop=(c == 31)), inc=("T", 1))
            tT = P.cur("T")
            P.wait("vector", "T", tT)
            P.op("vector", lambda e, cob=cob, NS=NS: e.tensor_tensor(
                out=yfm[:, cob * NS: cob * NS + NS], in0=pw1[:, 0:NS],
                in1=b2sl[:, 0:NS], op=ALU.mult))
        for b in range(nb):
            prb = b % 2
            P.wait("gpsimd", "V")
            P.op("gpsimd", lambda e, b=b, s0=s0, prb=prb: e.dma_start(
                out=x1rl[:, prb * 1024:(prb + 1) * 1024],
                in_=t["d_r1"][(s0 + b * 128):(s0 + (b + 1) * 128), :]), inc=("dX0", 16))
            dx = P.cur("dX0")
            P.wait("tensor", "V")
            P.wait("tensor", "S")
            for cob in range(8):
                P.op("tensor", lambda e, cob=cob, b=b, NS=NS: e.matmul(
                    out=pw0[:, cob * 128:(cob + 1) * 128],
                    lhsT=yfm[:, cob * NS + b * 128: cob * NS + (b + 1) * 128],
                    rhs=t["ident"].ap()[:, :], start=True, stop=True), inc=("T", 1))
            tT = P.cur("T")
            P.wait("vector", "T", tT)
            P.wait("vector", "dX0", dx)
            P.op("vector", lambda e, prb=prb: e.tensor_tensor(
                out=x2b[:, :], in0=pw0[:, 0:1024], in1=x1rl[:, prb * 1024:(prb + 1) * 1024],
                op=ALU.add))
            P.op("vector", lambda e: e.tensor_reduce(
                out=st2[:, 12:13], in_=x2b[:, :], axis=AX.X, op=ALU.add))
            vV = P.cur("V")
            P.wait("scalar", "V", vV)
            P.op("scalar", lambda e: e.activation(
                out=pw0[:, 1024:2048], in_=x2b[:, :], func=AF.Square, accum_out=st2[:, 13:14]))
            sS = P.cur("S")
            P.wait("vector", "S", sS)
            P.op("vector", lambda e: e.tensor_scalar_mul(
                out=st2[:, 12:13], in0=st2[:, 12:13], scalar1=1.0 / 1024.0))
            P.op("vector", lambda e: e.tensor_scalar_mul(
                out=st2[:, 13:14], in0=st2[:, 13:14], scalar1=1.0 / 1024.0))
            P.op("vector", lambda e: e.tensor_tensor(
                out=st2[:, 14:15], in0=st2[:, 12:13], in1=st2[:, 12:13], op=ALU.mult))
            P.op("vector", lambda e: e.tensor_tensor(
                out=st2[:, 14:15], in0=st2[:, 13:14], in1=st2[:, 14:15], op=ALU.subtract))
            P.op("vector", lambda e: e.tensor_scalar_add(
                out=st2[:, 14:15], in0=st2[:, 14:15], scalar1=1e-5))
            vV = P.cur("V")
            P.wait("scalar", "V", vV)
            P.op("scalar", lambda e: e.activation(
                out=st2[:, 15:16], in_=st2[:, 14:15], func=AF.Sqrt))
            sS = P.cur("S")
            P.wait("vector", "S", sS)
            P.op("vector", lambda e: e.reciprocal(out=st2[:, 15:16], in_=st2[:, 15:16]))
            P.op("vector", lambda e: e.tensor_tensor(
                out=st2[:, 14:15], in0=st2[:, 12:13], in1=st2[:, 15:16], op=ALU.mult))
            P.op("vector", lambda e: e.tensor_scalar_mul(
                out=st2[:, 14:15], in0=st2[:, 14:15], scalar1=-1.0))
            P.wait("vector", "dS")
            P.op("vector", lambda e, prb=prb: e.tensor_scalar(
                out=outb[:, prb * 1024:(prb + 1) * 1024], in0=x2b[:, :],
                scalar1=st2[:, 15:16], scalar2=st2[:, 14:15], op0=ALU.mult, op1=ALU.add))
            vV = P.cur("V")
            P.wait("gpsimd", "V", vV)
            P.op("gpsimd", lambda e, b=b, s0=s0, prb=prb: e.dma_start(
                out=t["out"][(s0 + b * 128):(s0 + (b + 1) * 128), :],
                in_=outb[:, prb * 1024:(prb + 1) * 1024]), inc=("dS", 16))
    P.wait("gpsimd", "dS")
    ar.release()
    ar.release()


def build_nc():
    nc = bass.Bass("TRN2", target_bir_lowering=False, debug=False)
    tt = {}
    tt["xb"] = nc.dram_tensor("xb", [T, C], F32, kind="ExternalInput").ap()
    tt["xres"] = nc.dram_tensor("xres", [NQMAX, C], F32, kind="ExternalInput").ap()
    tt["wattnT"] = nc.dram_tensor("wattnT", [C, 3 * C], F32, kind="ExternalInput").ap()
    tt["wprojT"] = nc.dram_tensor("wprojT", [C, C], F32, kind="ExternalInput").ap()
    tt["w1T"] = nc.dram_tensor("w1T", [C, 4096], F32, kind="ExternalInput").ap()
    tt["w2T"] = nc.dram_tensor("w2T", [4096, C], F32, kind="ExternalInput").ap()
    tt["convw"] = nc.dram_tensor("convw", [128, 96], F32, kind="ExternalInput").ap()
    tt["convb"] = nc.dram_tensor("convb", [128, 32], F32, kind="ExternalInput").ap()
    tt["identf"] = nc.dram_tensor("identf", [128, 128], F32, kind="ExternalInput").ap()
    tt["bonesf"] = nc.dram_tensor("bonesf", [128, 128], F32, kind="ExternalInput").ap()
    tt["masks"] = nc.dram_tensor("masks", [16, 5, 128, 512], BF16, kind="ExternalInput").ap()
    tt["mmasks"] = nc.dram_tensor("mmasks", [16, 128, 28], BF16, kind="ExternalInput").ap()
    tt["vonesd"] = nc.dram_tensor("vones", [128, 33], BF16, kind="ExternalInput").ap()
    tt["eind4f"] = nc.dram_tensor("eind4f", [16, 16], F32, kind="ExternalInput").ap()
    tt["eind44f"] = nc.dram_tensor("eind44f", [4, 512], F32, kind="ExternalInput").ap()
    tt["rank"] = nc.dram_tensor("rank", [1, 1], I32, kind="ExternalInput").ap()
    tt["out"] = nc.dram_tensor("out", [NQMAX, C], F32, kind="ExternalOutput").ap()

    tt["d_qT"] = nc.dram_tensor("d_qT", [C, NQMAX], BF16, kind="Internal").ap()
    d_Ks_h = nc.dram_tensor("d_Ks", [16, 64, SMAX], BF16, kind="Internal")
    d_Vs_h = nc.dram_tensor("d_Vs", [16, SMAX, 64], BF16, kind="Internal")
    tt["d_Ks"] = d_Ks_h.ap()
    tt["d_Vs"] = d_Vs_h.ap()
    tt["d_KsF"] = d_Ks_h.ap().flatten()
    tt["d_VsF"] = d_Vs_h.ap().flatten()
    tt["d_vtailT"] = nc.dram_tensor("d_vtailT", [C, NQMAX], BF16, kind="Internal").ap()
    if KSTOP == 0 or KSTOP >= 20:
        tt["d_ao"] = nc.dram_tensor("d_ao", [NQMAX, C], BF16, kind="Internal").ap()
    tt["d_xq"] = nc.dram_tensor("d_xq", [C, NQMAX], BF16, kind="Internal").ap()
    tt["d_r1"] = nc.dram_tensor("d_r1", [NQMAX, C], BF16, kind="Internal").ap()
    tt["d_w1t"] = nc.dram_tensor("d_w1t", [C, 4096], BF16, kind="Internal").ap()
    tt["d_w2t"] = nc.dram_tensor("d_w2t", [4096, C], BF16, kind="Internal").ap()

    if KDEBUG:
        tt["dbg_Ks"] = nc.dram_tensor("dbg_Ks", [16, 64, SMAX], F32, kind="ExternalOutput").ap()
        tt["dbg_Vs"] = nc.dram_tensor("dbg_Vs", [16, SMAX, 64], F32, kind="ExternalOutput").ap()
        tt["dbg_qT"] = nc.dram_tensor("dbg_qT", [C, NQMAX], F32, kind="ExternalOutput").ap()
        tt["dbg_vt"] = nc.dram_tensor("dbg_vt", [C, NQMAX], F32, kind="ExternalOutput").ap()
        tt["dbg_at"] = nc.dram_tensor("dbg_at", [16, NQMAX], F32, kind="ExternalOutput").ap()
        tt["dbg_ao"] = nc.dram_tensor("dbg_ao", [NQMAX // 128, 128, C], F32, kind="ExternalOutput").ap()
        tt["dbg_crow"] = nc.dram_tensor("dbg_crow", [1, 4 * NQMAX], F32, kind="ExternalOutput").ap()
        tt["dbg_cq4"] = nc.dram_tensor("dbg_cq4", [4, NQMAX], F32, kind="ExternalOutput").ap()
        tt["dbg_Vq"] = nc.dram_tensor("dbg_Vq", [128, 9216], F32, kind="ExternalOutput").ap()
        tt["dbg_zall"] = nc.dram_tensor("dbg_zall", [128, 128], F32, kind="ExternalOutput").ap()
        tt["dbg_atall"] = nc.dram_tensor("dbg_atall", [128, 128], F32, kind="ExternalOutput").ap()
        tt["dbg_vtall"] = nc.dram_tensor("dbg_vtall", [128, 8 * 512], F32, kind="ExternalOutput").ap()
        tt["dbg_vtq"] = nc.dram_tensor("dbg_vtq", [128, 2 * NQMAX], F32, kind="ExternalOutput").ap()

    from contextlib import ExitStack
    with ExitStack() as ctx:
        e = ctx.enter_context
        arena_h = e(nc.sbuf_tensor("arena", [128, 170 * 1024], mybir.dt.uint8))
        att_h = e(nc.sbuf_tensor("att_tail", [16, NQMAX], F32))
        ident_h = e(nc.sbuf_tensor("identb", [128, 128], BF16))
        bones_h = e(nc.sbuf_tensor("bonesb", [128, 128], BF16))
        kvb_h = e(nc.sbuf_tensor("kvb", [128, 1024], BF16))
        rank_h = e(nc.sbuf_tensor("rankb", [1, 1], I32))
        vones_h = e(nc.sbuf_tensor("vonesb", [128, 33], BF16))
        eind4_h = e(nc.sbuf_tensor("eind4b", [16, 16], F32))
        eind44_h = e(nc.sbuf_tensor("eind44b", [4, 512], F32))
        identf32_h = e(nc.sbuf_tensor("identf32b", [128, 128], F32))
        pw0_h = e(nc.psum_tensor("pw0", [128, 2048], F32))
        pw1_h = e(nc.psum_tensor("pw1", [128, 2048], F32))
        sems = {}
        for s in ("dI", "dG", "dS", "dW", "dX0", "dX1", "dK", "dV", "dVT", "dQ", "dZ", "dC", "T", "V", "S", "G"):
            sems[s] = e(nc.semaphore(s))
        block = e(nc.Block())

        plans = {}
        for s in KBODIES:
            tdict = dict(tt)
            tdict["ar"] = Arena(arena_h, 170 * 1024)
            tdict["att_tail"] = att_h
            tdict["ident"] = ident_h
            tdict["bones"] = bones_h
            tdict["kvb"] = kvb_h.ap()
            tdict["vones"] = vones_h.ap()
            tdict["eind4"] = eind4_h
            tdict["eind44"] = eind44_h
            tdict["identf32"] = identf32_h
            tdict["pw0"] = pw0_h.ap()
            tdict["pw1"] = pw1_h.ap()
            tdict["xfree"] = [0, 0]
            tdict["xtS"] = [0, 0]
            tdict["qkvV"] = [0, 0]
            tdict["qkvS"] = [0, 0]
            tdict["kvS"] = [0, 0]
            tdict["tailV"] = [0, 0]
            tdict["vtpS"] = 0
            tdict["attS"] = 0
            tdict["attT"] = 0
            tdict["prodV"] = 0
            tdict["qF"] = 0
            tdict["vt8F"] = 0
            tdict["vtpF"] = [0, 0]
            tdict["stgF"] = {(0, 0): 0, (0, 1): 0, (1, 0): 0, (1, 1): 0}
            Pl = Plan()
            cfg = slice_cfg(s)
            emit_qkv(Pl, cfg, tdict)
            if KSTOP == 0 or KSTOP >= 20:
                emit_attention(Pl, cfg, tdict)
            if KSTOP == 0:
                emit_ffn(Pl, cfg, tdict)
            if KDEBUG:
                emit_debug_dump(Pl, cfg, tdict)
            plans[s] = Pl

        @block.sync
        def _(sync):
            sync.dma_start(out=rank_h.ap()[:, :], in_=tt["rank"]).then_inc(sems["dI"], 16)
            for s in KBODIES:
                assert not plans[s].ops["sync"], "sync engine must stay branch-free"

        for name, dec in (("tensor", block.tensor), ("vector", block.vector),
                          ("scalar", block.scalar), ("gpsimd", block.gpsimd)):
            def f(eng, name=name):
                rr = eng.to_reg(0)
                eng.wait_ge(sems["dI"], 16)
                rr = eng.to_reg(0)
                eng.reg_load(rr, rank_h.ap()[0:1, 0:1])
                for s in KBODIES:
                    with eng.If_eq(rr, s):
                        plans[s].replay(name, eng, sems)
            dec(f)
    return nc


def emit_debug_dump(P, cfg, t):
    for sname in ("dK", "dV", "dVT", "dQ"):
        P.wait("gpsimd", sname)
    P.wait("gpsimd", "V")
    P.wait("gpsimd", "S")
    P.op("gpsimd", lambda e: e.dma_start(out=t["dbg_Ks"], in_=t["d_Ks"]), inc=("dG", 16))
    P.op("gpsimd", lambda e: e.dma_start(out=t["dbg_Vs"], in_=t["d_Vs"]), inc=("dG", 16))
    nq = cfg["nq"]
    P.op("gpsimd", lambda e, nq=nq: e.dma_start(out=t["dbg_qT"][:, 0:nq], in_=t["d_qT"][:, 0:nq]), inc=("dG", 16))
    P.op("gpsimd", lambda e, nq=nq: e.dma_start(out=t["dbg_vt"][:, 0:nq], in_=t["d_vtailT"][:, 0:nq]), inc=("dG", 16))
    P.op("gpsimd", lambda e, nq=nq: e.dma_start(out=t["dbg_at"][:, 0:nq], in_=t["att_tail"].ap()[:, 0:nq]), inc=("dG", 16))
    if "attn_out" in t:
        P.op("gpsimd", lambda e: e.dma_start(out=t["dbg_crow"][:, 0:4 * cfg["nq"]],
                                             in_=t["qTq_dbg"][64:65, :]), inc=("dG", 16))
        P.op("gpsimd", lambda e: e.dma_start(out=t["dbg_cq4"][:, 0:cfg["nq"]],
                                             in_=t["cq4_dbg"][0:4, :]), inc=("dG", 16))
        if "Vq_dbg" in t:
            P.op("gpsimd", lambda e: e.dma_start(out=t["dbg_Vq"][:, 0:t["Vq_cols"]],
                                                 in_=t["Vq_dbg"]), inc=("dG", 16))
        if "vtailq_dbg" in t:
            P.op("gpsimd", lambda e: e.dma_start(out=t["dbg_vtq"][:, 0:t["vtailq_cols"]],
                                                 in_=t["vtailq_dbg"]), inc=("dG", 16))
        if "zall_dbg" in t:
            P.op("gpsimd", lambda e: e.dma_start(out=t["dbg_zall"][:, 0:t["zall_cols"]],
                                                 in_=t["zall_dbg"]), inc=("dG", 16))
            P.op("gpsimd", lambda e: e.dma_start(out=t["dbg_atall"][:, 0:t["zall_cols"]],
                                                 in_=t["atall_dbg"]), inc=("dG", 16))
            P.op("gpsimd", lambda e: e.dma_start(out=t["dbg_vtall"][:, 0:t["vtall_cols"]],
                                                 in_=t["vtall_dbg"]), inc=("dG", 16))
        ao = t["attn_out"]
        for ti in range(t["attn_ntq"]):
            P.op("gpsimd", lambda e, ti=ti, ao=ao: e.dma_start(
                out=t["dbg_ao"][ti], in_=ao[:, ti * 1024:(ti + 1) * 1024]), inc=("dG", 16))
    P.op("gpsimd", lambda e: e.dma_start(out=t["out"], in_=t["xres"]), inc=("dG", 16))
    P.wait("gpsimd", "dG")


_NC_CACHE = {}


def _get_nc():
    key = (KDEBUG, tuple(KBODIES))
    if key not in _NC_CACHE:
        _NC_CACHE[key] = build_nc()
    return _NC_CACHE[key]


def make_inputs(L0_attn_x, L0_attn_W_attn, L0_attn_W_proj, L0_ffn_w1, L0_ffn_conv_w,
                L0_ffn_conv_b, L0_ffn_w2):
    x = np.ascontiguousarray(np.asarray(L0_attn_x, dtype=np.float32))
    bf = ml_dtypes.bfloat16
    wattnT = np.ascontiguousarray(np.asarray(L0_attn_W_attn, np.float32).T)
    wprojT = np.ascontiguousarray(np.asarray(L0_attn_W_proj, np.float32).T)
    w1T = np.ascontiguousarray(np.asarray(L0_ffn_w1, np.float32).T)
    w2T = np.ascontiguousarray(np.asarray(L0_ffn_w2, np.float32).T)
    cw = np.asarray(L0_ffn_conv_w, np.float32).reshape(4096, 3)
    convw = np.zeros((128, 96), np.float32)
    for cchunk in range(32):
        for j in range(3):
            convw[:, 3 * cchunk + j] = cw[128 * cchunk:128 * (cchunk + 1), j]
    cb = np.asarray(L0_ffn_conv_b, np.float32)
    convb = np.zeros((128, 32), np.float32)
    for cchunk in range(32):
        convb[:, cchunk] = cb[128 * cchunk:128 * (cchunk + 1)]
    identf = np.eye(128, dtype=np.float32)
    eind44 = np.zeros((4, 512), np.float32)
    for hl in range(4):
        eind44[hl, hl * 128:(hl + 1) * 128] = 1.0
    bonesf = np.zeros((128, 128), np.float32)
    for pr in range(8):
        bonesf[0:64, pr * 16 + 2 * pr] = 1.0
        bonesf[64:128, pr * 16 + 2 * pr + 1] = 1.0
    in_maps = []
    for c in range(8):
        b, s = c // 4, c % 4
        cfg = slice_cfg(s)
        xres = np.zeros((NQMAX, C), np.float32)
        xres[0:cfg["nq"]] = x[b, cfg["Q0"]:cfg["Q1"]]
        vones = np.zeros((128, 33), np.float32)
        r, real = set_r_index(cfg["NB"], cfg["NU"])
        for row in range(60 * cfg["NB"]):
            vones[row % 128, row // 128] = 1.0
        for u in range(cfg["NU"]):
            for j, L, i in r2_slots(u):
                row = R2B + 19 * u + j
                vones[row % 128, row // 128] = 1.0
        in_maps.append({
            "xb": x[b], "xres": xres, "wattnT": wattnT, "wprojT": wprojT,
            "w1T": w1T, "w2T": w2T, "convw": convw, "convb": convb,
            "identf": identf, "bonesf": bonesf, "masks": np.asarray(MASK_DATA),
            "mmasks": np.asarray(MMASK),
            "eind4f": np.eye(16, dtype=np.float32),
            "eind44f": eind44,
            "vones": vones.astype(bf), "rank": np.array([[s]], np.int32),
        })
    return in_maps


def kernel(**inputs):
    in_maps = make_inputs(**inputs)
    nc = _get_nc()
    res = run_bass_kernel_spmd(nc, in_maps, core_ids=list(range(8)))
    out = np.zeros((2, T, C), np.float32)
    for c in range(8):
        b, s = c // 4, c % 4
        cfg = slice_cfg(s)
        o = res.results[c]["out"]
        off = cfg["T0"] - cfg["Q0"]
        out[b, cfg["T0"]:cfg["T1"]] = o[off: off + (cfg["T1"] - cfg["T0"])]
    if KDEBUG:
        kernel.last_results = res
    return out


# version 04
# speedup vs baseline: 1.2086x; 1.2086x over previous
import os
import numpy as np
import ml_dtypes
import concourse.bass as bass
import concourse.mybir as mybir
from concourse.bass_utils import run_bass_kernel_spmd

F32 = mybir.dt.float32
BF16 = mybir.dt.bfloat16
I32 = mybir.dt.int32
AF = mybir.ActivationFunctionType
ALU = mybir.AluOpType
AX = mybir.AxisListType

T = 8192
C = 1024
TB = (0, 3072, 5120, 6656, 8192)
R2B = 3840
SMAX = 4224
NQMAX = 3200

KDEBUG = int(os.environ.get("KDEBUG", "0"))
KTILES = int(os.environ.get("KTILES", "0"))
KSTOP = int(os.environ.get("KSTOP", "0"))
KSTOPA = int(os.environ.get("KSTOPA", "0"))
KSKIPR2 = int(os.environ.get("KSKIPR2", "0"))
KR2S = int(os.environ.get("KR2S", "0"))
KBODIES = [int(b) for b in os.environ.get("KBODIES", "0,1,2,3").split(",")]


def slice_cfg(s):
    T0, T1 = TB[s], TB[s + 1]
    Q0 = max(0, T0 - 128)
    Q1 = min(T, T1 + 128)
    P1 = min(T, T1 + 512)
    return dict(s=s, T0=T0, T1=T1, Q0=Q0, Q1=Q1, P1=P1,
                NB=P1 // 128, NU=P1 // 512, nq=Q1 - Q0,
                G0=Q0 // 512, G1=(Q1 - 1) // 512)


def r2_slots(u):
    out = []
    for j in range(8):
        out.append((j, 64, 8 * u + j))
    for j in range(4):
        out.append((8 + j, 128, 4 * u + j))
    for j in range(2):
        out.append((12 + j, 256, 2 * u + j))
    out.append((14, 512, u))
    for j, L in ((15, 1024), (16, 2048), (17, 4096), (18, 8192)):
        if (512 * (u + 1)) % L == 0:
            out.append((j, L, (512 * u) // L))
    return out


def set_r_index(NBmax=64, NUmax=16):
    r = np.full(SMAX, 10 ** 9, dtype=np.int64)
    real = np.zeros(SMAX, dtype=bool)
    for b in range(NBmax):
        off, pos = 60 * b, 0
        for L, cnt in ((4, 32), (8, 16), (16, 8), (32, 4)):
            for i in range(cnt):
                r[off + pos] = 128 * b + (32 * b // (L // 4)) * 0 + i * L + L - 1
                real[off + pos] = True
                pos += 1
    for u in range(NUmax):
        off = R2B + 19 * u
        for j, L, i in r2_slots(u):
            r[off + j] = i * L + L - 1
            real[off + j] = True
    return r, real


def host_masks():
    r, _ = set_r_index()
    data = np.ones((16, 5, 128, 512), dtype=np.float32)
    meta = []
    for G in range(16):
        chunks = list(range((240 * G) // 128, (240 * (G + 1) + 127) // 128))
        chunks += list(range((R2B + 19 * G) // 128, (R2B + 19 * (G + 1) + 127) // 128))
        assert len(chunks) <= 5
        meta.append(chunks)
        tq = 512 * G + np.arange(512)
        for k, cc in enumerate(chunks):
            rows = np.arange(128 * cc, 128 * cc + 128)
            lo, hi = ((240 * G, 240 * (G + 1)) if cc < 30
                      else (R2B + 19 * G, R2B + 19 * (G + 1)))
            m = np.ones((128, 512), dtype=np.float32)
            bnd = (rows >= lo) & (rows < hi)
            m[bnd] = (r[rows][bnd][:, None] <= tq[None, :]).astype(np.float32)
            m[rows >= hi] = 0.0
            data[G, k] = (m - 1.0) * 4000.0
    return data.astype(ml_dtypes.bfloat16), meta


MASK_DATA, MASK_META = host_masks()


def host_mmask():
    r, real = set_r_index()
    mm = np.zeros((16, 128, 28), dtype=np.float32)
    for G in range(16):
        for j in range(7):
            row = R2B + 19 * G + 12 + j
            for ti in range(4):
                tq = 512 * G + 128 * ti + np.arange(128)
                mm[G, :, ti * 7 + j] = (real[row] & (r[row] <= tq)).astype(np.float32)
    return mm.astype(ml_dtypes.bfloat16)


MMASK = host_mmask()

ENGS = ("sync", "tensor", "vector", "scalar", "gpsimd")


class Plan:
    def __init__(self):
        self.ops = {e: [] for e in ENGS}
        self.sem = {}

    def cur(self, sem):
        return self.sem.get(sem, 0)

    def bump(self, sem, n):
        self.sem[sem] = self.cur(sem) + n
        return self.sem[sem]

    ENGSEM = {"vector": "V", "scalar": "S", "gpsimd": "G"}

    def op(self, eng, fn, inc=None, dma=False):
        incs = [inc] if isinstance(inc, tuple) else list(inc or [])
        dma = dma or any(s.startswith("d") for s, _ in incs)
        auto = eng in self.ENGSEM and not dma
        if auto:
            assert not incs, "compute ops on V/S/G auto-inc their engine sem"
            self.bump(self.ENGSEM[eng], 1)
        for s, n in incs:
            self.bump(s, n)
        self.ops[eng].append(("op", fn, incs, dma, auto))

    def wait(self, eng, sem, val=None):
        v = self.cur(sem) if val is None else val
        self.ops[eng].append(("wait", sem, v))
        return v

    def barrier(self, sems=("T", "V", "S", "G", "dK", "dV", "dVT", "dQ", "dG", "dW", "dZ", "dX0", "dX1")):
        for e in ("tensor", "vector", "scalar", "gpsimd"):
            for s in sems:
                if self.cur(s):
                    self.wait(e, s)

    def replay(self, name, eng, semh):
        selfsem = self.ENGSEM.get(name)
        cnt = 0
        for it in self.ops[name]:
            if it[0] == "wait":
                eng.wait_ge(semh[it[1]], it[2])
            else:
                _, fn, incs, dma, auto = it
                if selfsem and cnt > 0:
                    eng.wait_ge(semh[selfsem], cnt)
                inst = fn(eng)
                if auto:
                    inst.then_inc(semh[selfsem], 1)
                    cnt += 1
                for s, n in incs:
                    inst.then_inc(semh[s], n)


class Arena:

    def __init__(self, handle, nbytes):
        self.h = handle
        self.nbytes = nbytes
        self.off = 0
        self.marks = []

    def alloc(self, dtype, elems, parts=128):
        esz = 4 if dtype == F32 else 2
        self.off = (self.off + esz - 1) // esz * esz
        a = self.off
        self.off += elems * esz
        assert self.off <= self.nbytes, f"arena overflow {self.off} > {self.nbytes}"
        if dtype == F32:
            return self.h.bitcast(F32).ap()[0:parts, a // 4: a // 4 + elems]
        return self.h.bitcast(BF16).ap()[0:parts, a // 2: a // 2 + elems]

    def mark(self):
        self.marks.append(self.off)

    def release(self):
        self.off = self.marks.pop()


def r3o(view, off, inner, count, stride):
    assert off + inner <= stride
    return view[:, 0:stride * count].rearrange("p (b w) -> p b w", w=stride)[:, :, off:off + inner]


def emit_qkv(P, cfg, t):
    NB, NU, P1, Q0, Q1 = cfg["NB"], cfg["NU"], cfg["P1"], cfg["Q0"], cfg["Q1"]
    ar = t["ar"]
    pw0, pw1 = t["pw0"], t["pw1"]
    ar.mark()

    watf = ar.alloc(BF16, 8 * 3072)
    xstage = ar.alloc(BF16, 2 * 4 * 1024)
    xT = ar.alloc(BF16, 8 * 512)
    s2b = ar.alloc(F32, 256)
    s4b = ar.alloc(F32, 128)
    s8b = ar.alloc(F32, 64)
    s16b = ar.alloc(F32, 32)
    l32 = ar.alloc(F32, 16 * 256)
    stg = ar.alloc(BF16, 2 * 240)
    vst = ar.alloc(BF16, 2 * 240)
    vtp = ar.alloc(BF16, 2 * 256)
    kt8 = ar.alloc(BF16, 8 * 512)
    vt8 = ar.alloc(BF16, 512)
    prodb = ar.alloc(BF16, 512)
    qstg = ar.alloc(BF16, 512)
    attb = t["att_tail"]

    ntile = P1 // 512
    if KTILES:
        ntile = min(ntile, KTILES)
    own_lo, own_hi = Q0 // 512, (Q1 + 511) // 512

    zb = ar.alloc(BF16, 512 * 2)
    P.op("vector", lambda e: e.memset(zb, 0.0))
    P.wait("gpsimd", "V")
    flatK = t["d_KsF"]
    flatV = t["d_VsF"]
    nz = (16 * 64 * SMAX + 128 * 1024 - 1) // (128 * 1024)
    for z in range(nz):
        n = min(128 * 1024, 16 * 64 * SMAX - z * 128 * 1024)
        P.op("gpsimd", lambda e, z=z, n=n, flatK=flatK: e.dma_start(
            out=flatK[z * 128 * 1024: z * 128 * 1024 + n].rearrange("(p w) -> p w", p=128),
            in_=zb[:, 0:n // 128]), inc=("dZ", 16), dma=True)
        P.op("gpsimd", lambda e, z=z, n=n, flatV=flatV: e.dma_start(
            out=flatV[z * 128 * 1024: z * 128 * 1024 + n].rearrange("(p w) -> p w", p=128),
            in_=zb[:, 0:n // 128]), inc=("dZ", 16), dma=True)
    P.wait("gpsimd", "dZ")
    if KSTOP == 1:
        ar.release()
        return

    for cc in range(8):
        P.op("gpsimd", lambda e, cc=cc: e.dma_start(
            out=watf[:, cc * 3072:(cc + 1) * 3072],
            in_=t["wattnT"][128 * cc:128 * (cc + 1), :]), inc=("dW", 16))
    P.op("gpsimd", lambda e: e.dma_start(out=t["ident"].ap()[:, :], in_=t["identf"]), inc=("dW", 16))
    P.op("gpsimd", lambda e: e.dma_start(out=t["bones"].ap()[:, :], in_=t["bonesf"]), inc=("dW", 16))
    P.op("gpsimd", lambda e: e.dma_start(out=t["vones"], in_=t["vonesd"]), inc=("dW", 16))
    P.op("gpsimd", lambda e: e.dma_start(out=t["eind4"].ap()[:, :], in_=t["eind4f"]), inc=("dW", 16))
    P.op("gpsimd", lambda e: e.dma_start(out=t["eind44"].ap()[:, :], in_=t["eind44f"]), inc=("dW", 16))
    P.op("gpsimd", lambda e: e.dma_start(out=t["identf32"].ap()[:, :], in_=t["identf"]), inc=("dW", 16))
    P.wait("tensor", "dW")
    P.wait("vector", "dW")
    P.wait("scalar", "dW")
    if KSTOP == 2:
        ar.release()
        return

    for ti in range(ntile):
        is_own = own_lo <= ti < own_hi
        tok0 = 512 * ti
        oa = max(0, Q0 - tok0)
        ow = max(0, min(tok0 + 512, Q1) - max(tok0, Q0))
        xs = xstage[:, (ti % 2) * 4096:(ti % 2) * 4096 + 4096]
        if ti >= 2:
            P.wait("gpsimd", "T", t["xfree"][ti % 2])
        dx = "dX%d" % (ti % 2)
        for st in range(4):
            P.op("gpsimd", lambda e, st=st, tok0=tok0, xs=xs: e.dma_start(
                out=xs[:, st * 1024:(st + 1) * 1024],
                in_=t["xb"][tok0 + 128 * st: tok0 + 128 * (st + 1), :]), inc=(dx, 16), dma=True)
        P.wait("tensor", dx)

        for st in range(4):
            pb = pw0[:, (st % 2) * 1024:(st % 2) * 1024 + 1024]
            if ti > 0 or st >= 2:
                P.wait("tensor", "S", t["xtS"][st % 2])
            for ccc in range(8):
                P.op("tensor", lambda e, xs=xs, st=st, ccc=ccc, pb=pb: e.matmul(
                    out=pb[:, (ccc % 4) * 128 + (ccc // 4) * 512:(ccc % 4) * 128 + (ccc // 4) * 512 + 128],
                    lhsT=xs[:, st * 1024 + ccc * 128: st * 1024 + (ccc + 1) * 128],
                    rhs=t["ident"].ap()[:, :], start=True, stop=True),
                    inc=("T", 1) if ccc == 7 else None)
            P.wait("scalar", "T")
            for half in range(2):
                P.op("scalar", lambda e, pb=pb, half=half, st=st: e.copy(
                    out=r3o(xT[:, half * 4 * 512:], st * 128, 128, 4, 512),
                    in_=pb[:, half * 512: half * 512 + 512].rearrange("p (b w) -> p b w", w=128)),
                    )
            t["xtS"][st % 2] = P.cur("S")
        t["xfree"][ti % 2] = P.cur("T")
        P.wait("tensor", "S")
        if KSTOP == 3:
            continue

        fchunks = list(range(8, 24)) + (list(range(0, 8)) if is_own and KSTOP != 4 else [])
        if KSTOP == 5:
            fchunks = fchunks[:2]
        if KSTOP == 6:
            fchunks = [8, 16]
        if KSTOP == 7:
            fchunks = list(range(8, 12))
        if KSTOP == 8:
            fchunks = list(range(8, 24))[:8]
        for j, fc in enumerate(fchunks):
            slot = j % 2
            bank = pw1[:, slot * 512: slot * 512 + 512]
            P.wait("tensor", "V", t["qkvV"][slot])
            P.wait("tensor", "S", t["qkvS"][slot])
            for ccc in range(8):
                P.op("tensor", lambda e, fc=fc, ccc=ccc, bank=bank: e.matmul(
                    out=bank, lhsT=watf[:, ccc * 3072 + fc * 128: ccc * 3072 + fc * 128 + 128],
                    rhs=xT[:, ccc * 512:(ccc + 1) * 512],
                    start=(ccc == 0), stop=(ccc == 7)), inc=("T", 1) if ccc == 7 else None)
            tsig = P.cur("T")

            if fc >= 8:
                is_v = fc >= 16
                pr = fc - 16 if is_v else fc - 8
                stv = vst if is_v else stg
                sbuf_stg = stv[:, (ti % 2) * 240:(ti % 2) * 240 + 240]
                P.wait("vector", "T", tsig)
                if is_v:
                    P.wait("vector", "T", t["stgF"][(ti % 2, 1)])
                else:
                    P.wait("vector", "dK")
                P.op("vector", lambda e, bank=bank: e.tensor_reduce(
                    out=s4b, in_=bank.rearrange("p (b w) -> p b w", w=4),
                    axis=AX.X, op=ALU.add))
                P.op("vector", lambda e, sbuf_stg=sbuf_stg: e.tensor_copy(
                    out=r3o(sbuf_stg, 0, 32, 4, 60), in_=s4b.rearrange("p (b w) -> p b w", w=32)))
                P.op("vector", lambda e: e.tensor_tensor(
                    out=s8b, in0=s4b[:, 0:128:2], in1=s4b[:, 1:128:2], op=ALU.add))
                P.op("vector", lambda e, sbuf_stg=sbuf_stg: e.tensor_copy(
                    out=r3o(sbuf_stg, 32, 16, 4, 60), in_=s8b.rearrange("p (b w) -> p b w", w=16)))
                P.op("vector", lambda e: e.tensor_tensor(
                    out=s16b, in0=s8b[:, 0:64:2], in1=s8b[:, 1:64:2], op=ALU.add))
                P.op("vector", lambda e, sbuf_stg=sbuf_stg: e.tensor_copy(
                    out=r3o(sbuf_stg, 48, 8, 4, 60), in_=s16b.rearrange("p (b w) -> p b w", w=8)))
                kvi = pr + (8 if is_v else 0)
                P.op("vector", lambda e, kvi=kvi, ti=ti: e.tensor_tensor(
                    out=l32[:, kvi * 256 + ti * 16: kvi * 256 + ti * 16 + 16],
                    in0=s16b[:, 0:32:2], in1=s16b[:, 1:32:2], op=ALU.add))
                P.op("vector", lambda e, sbuf_stg=sbuf_stg, kvi=kvi, ti=ti: e.tensor_copy(
                    out=r3o(sbuf_stg, 56, 4, 4, 60),
                    in_=l32[:, kvi * 256 + ti * 16: kvi * 256 + ti * 16 + 16].rearrange("p (b w) -> p b w", w=4)))
                t["qkvV"][slot] = P.cur("V")
                vsig_stage = P.cur("V")

                if is_own:
                    P.wait("scalar", "T", tsig)
                    P.wait("scalar", "V", vsig_stage)
                    P.wait("scalar", "V", t["tailV"][slot])
                    dst = t["kvb"][:, slot * 512: slot * 512 + 512]
                    P.op("scalar", lambda e, bank=bank, dst=dst: e.copy(out=dst, in_=bank))
                    t["qkvS"][slot] = P.cur("S")
                    ssig = P.cur("S")
                    P.wait("vector", "S", ssig)
                    if is_v:
                        P.wait("vector", "dVT")
                    tl = kt8[:, pr * 512:(pr + 1) * 512] if not is_v else vt8
                    P.op("vector", lambda e, tl=tl, dst=dst: e.tensor_copy(out=tl, in_=dst))
                    for m in (1, 2, 3):
                        P.op("vector", lambda e, tl=tl, dst=dst, m=m: e.tensor_tensor(
                            out=r3o(tl, m, 4 - m, 128, 4),
                            in0=r3o(tl, m, 4 - m, 128, 4),
                            in1=r3o(dst, 0, 4 - m, 128, 4), op=ALU.add),
                            dma=False)
                    t["tailV"][slot] = P.cur("V")
                    if is_v:
                        vs = P.cur("V")
                        P.wait("gpsimd", "V", vs)
                        P.op("gpsimd", lambda e, pr=pr, tok0=tok0, oa=oa, ow=ow: e.dma_start(
                            out=t["d_vtailT"][128 * pr:128 * (pr + 1), tok0 + oa - Q0: tok0 + oa - Q0 + ow],
                            in_=vt8[:, oa:oa + ow]), inc=("dVT", 16))
                        t["vt8F"] = P.cur("dVT")
                else:
                    t["qkvS"][slot] = P.cur("S")

                if is_v and KSTOP == 9:
                    pass
                elif is_v:
                    P.wait("tensor", "V", vsig_stage)
                    P.wait("tensor", "S", t["vtpS"])
                    P.wait("tensor", "S", t["attS"])
                    for h2 in range(2):
                        ptv = pw1[:, 1024 + h2 * 512:1536 + h2 * 512]
                        for cchunk in range(2):
                            w = 128 if cchunk == 0 else 112
                            P.op("tensor", lambda e, sbuf_stg=sbuf_stg, h2=h2, cchunk=cchunk, w=w, ptv=ptv: e.matmul(
                                out=ptv[0:w, cchunk * 64:cchunk * 64 + 64],
                                lhsT=sbuf_stg[h2 * 64:(h2 + 1) * 64, cchunk * 128: cchunk * 128 + w],
                                rhs=t["ident"].ap()[h2 * 64:(h2 + 1) * 64, h2 * 64:(h2 + 1) * 64],
                                start=True, stop=True),
                                inc=("T", 1) if (h2 == 1 and cchunk == 1) else None)
                    ttv = P.cur("T")
                    t["stgF"][(ti % 2, 1)] = ttv
                    if KSTOP == 10:
                        continue
                    P.wait("scalar", "T", ttv)
                    P.wait("scalar", "dV")
                    vtpb = vtp[:, (ti % 2) * 256:(ti % 2) * 256 + 256]
                    for h2 in range(2):
                        ptv = pw1[:, 1024 + h2 * 512:1536 + h2 * 512]
                        P.op("scalar", lambda e, ptv=ptv, vtpb=vtpb, h2=h2: e.copy(
                            out=vtpb[:, h2 * 128:h2 * 128 + 64], in_=ptv[:, 0:64]))
                        P.op("scalar", lambda e, ptv=ptv, vtpb=vtpb, h2=h2: e.copy(
                            out=vtpb[0:112, h2 * 128 + 64:h2 * 128 + 128], in_=ptv[0:112, 64:128]))
                    t["vtpS"] = P.cur("S")
                    svt = P.cur("S")
                    if KSTOP == 11:
                        continue
                    P.wait("gpsimd", "S", svt)
                    for h2 in range(2):
                        hh = 2 * pr + h2
                        P.op("gpsimd", lambda e, hh=hh, h2=h2, ti=ti, vtpb=vtpb: e.dma_start(
                            out=t["d_Vs"][hh, 240 * ti: 240 * ti + 128, :],
                            in_=vtpb[:, h2 * 128: h2 * 128 + 64]), inc=("dV", 16))
                        P.op("gpsimd", lambda e, hh=hh, h2=h2, ti=ti, vtpb=vtpb: e.dma_start(
                            out=t["d_Vs"][hh, 240 * ti + 128: 240 * ti + 240, :],
                            in_=vtpb[0:112, h2 * 128 + 64: h2 * 128 + 128]), inc=("dV", 16))
                    t["vtpF"][ti % 2] = P.cur("dV")
                else:
                    P.wait("gpsimd", "V", vsig_stage)
                    for h2 in range(2):
                        hh = 2 * pr + h2
                        P.op("gpsimd", lambda e, hh=hh, h2=h2, ti=ti, sbuf_stg=sbuf_stg: e.dma_start(
                            out=t["d_Ks"][hh, :, 240 * ti: 240 * ti + 240],
                            in_=sbuf_stg[h2 * 64:(h2 + 1) * 64, :]), inc=("dK", 16))
                    t["stgF"][(ti % 2, 0)] = P.cur("dK")
            else:
                pr = fc
                P.wait("scalar", "T", tsig)
                P.wait("scalar", "V", t["prodV"])
                P.wait("scalar", "dQ")
                P.op("scalar", lambda e, bank=bank: e.copy(out=qstg, in_=bank))
                t["qkvS"][slot] = P.cur("S")
                t["qkvV"][slot] = P.cur("V")
                sq = P.cur("S")
                P.wait("gpsimd", "S", sq)
                P.op("gpsimd", lambda e, pr=pr, tok0=tok0, oa=oa, ow=ow: e.dma_start(
                    out=t["d_qT"][128 * pr:128 * (pr + 1), tok0 + oa - Q0: tok0 + oa - Q0 + ow],
                    in_=qstg[:, oa:oa + ow]), inc=("dQ", 16))
                t["qF"] = P.cur("dQ")
                P.wait("vector", "S", sq)
                P.wait("vector", "T", t["attT"])
                P.op("vector", lambda e, pr=pr: e.tensor_tensor(
                    out=prodb, in0=qstg, in1=kt8[:, pr * 512:(pr + 1) * 512], op=ALU.mult))
                pv = P.cur("V")
                t["prodV"] = pv
                P.wait("tensor", "V", pv)
                if pr == 0:
                    P.wait("tensor", "S", t["attS"])
                    P.wait("tensor", "S", t["vtpS"])
                P.op("tensor", lambda e, pr=pr, oa=oa, ow=ow: e.matmul(
                    out=pw1[0:16, 1536:1536 + ow], lhsT=t["bones"].ap()[:, pr * 16:(pr + 1) * 16],
                    rhs=prodb[:, oa:oa + ow], start=(pr == 0), stop=(pr == 7),
                    skip_group_check=True), inc=("T", 1))
                t["attT"] = P.cur("T")
                if pr == 7:
                    ts2 = P.cur("T")
                    P.wait("scalar", "T", ts2)
                    P.op("scalar", lambda e, tok0=tok0, oa=oa, ow=ow: e.copy(
                        out=attb.ap()[:, tok0 + oa - Q0: tok0 + oa - Q0 + ow],
                        in_=pw1[0:16, 1536:1536 + ow]))
                    t["attS"] = P.cur("S")

    if KSKIPR2:
        P.op("gpsimd", lambda e: e.engine_nop())
        ar.release()
        return
    r2w = 19 * NU
    r2k = ar.alloc(BF16, 2 * 304)
    s64 = ar.alloc(F32, 128)
    s128 = ar.alloc(F32, 64)
    s256 = ar.alloc(F32, 32)
    s512 = ar.alloc(F32, 16)
    s1024 = ar.alloc(F32, 8)
    s2048 = ar.alloc(F32, 4)
    s4096 = ar.alloc(F32, 2)
    s8192 = ar.alloc(F32, 1)
    for kvi in range(16):
        is_v = kvi >= 8
        pr = kvi - 8 if is_v else kvi
        P.wait("vector", "dK")
        P.wait("vector", "dV")
        P.wait("vector", "T")
        base = l32[:, kvi * 256: kvi * 256 + NU * 16]
        rstg = r2k[:, (0 if not is_v else 304):(304 if not is_v else 608)][:, 0:r2w]
        P.op("vector", lambda e, rstg=rstg: e.memset(rstg, 0.0))
        n64 = NU * 8
        P.op("vector", lambda e, base=base, n64=n64: e.tensor_tensor(
            out=s64[:, 0:n64], in0=base[:, 0:2 * n64:2], in1=base[:, 1:2 * n64:2], op=ALU.add))
        P.op("vector", lambda e, rstg=rstg, n64=n64, NU=NU: e.tensor_copy(
            out=r3o(rstg, 0, 8, NU, 19), in_=s64[:, 0:n64].rearrange("p (b w) -> p b w", w=8)))
        if KR2S == 1:
            continue
        lv = [(s64, s128, 4, 8), (s128, s256, 2, 12), (s256, s512, 1, 14)]
        for src, dst, per, off in lv:
            n = NU * per
            P.op("vector", lambda e, src=src, dst=dst, n=n: e.tensor_tensor(
                out=dst[:, 0:n], in0=src[:, 0:2 * n:2], in1=src[:, 1:2 * n:2], op=ALU.add))
            P.op("vector", lambda e, rstg=rstg, dst=dst, n=n, per=per, off=off, NU=NU: e.tensor_copy(
                out=r3o(rstg, off, per, NU, 19), in_=dst[:, 0:n].rearrange("p (b w) -> p b w", w=per)))
        if KR2S == 2:
            continue
        hi = [(s512, s1024, 34, 1024, 38), (s1024, s2048, 73, 2048, 76),
              (s2048, s4096, 150, 4096, 152), (s4096, s8192, 303, 8192, 304)]
        for src, dst, off, L, stride in hi:
            n = P1 // L
            if n == 0:
                continue
            P.op("vector", lambda e, src=src, dst=dst, n=n: e.tensor_tensor(
                out=dst[:, 0:n], in0=src[:, 0:2 * n:2], in1=src[:, 1:2 * n:2], op=ALU.add))
            cnt = n
            P.op("vector", lambda e, rstg=rstg, dst=dst, cnt=cnt, off=off, stride=stride: e.tensor_copy(
                out=r3o(rstg, off, 1, cnt, stride), in_=dst[:, 0:cnt].rearrange("p (b w) -> p b w", w=1)))
        vs = P.cur("V")
        if KR2S == 3:
            continue
        if KR2S == 5 and not is_v:
            continue
        if KR2S == 4 and is_v:
            continue
        if not is_v:
            P.wait("gpsimd", "V", vs)
            for h2 in range(2):
                hh = 2 * pr + h2
                P.op("gpsimd", lambda e, hh=hh, h2=h2, rstg=rstg, r2w=r2w: e.dma_start(
                    out=t["d_Ks"][hh, :, R2B: R2B + r2w], in_=rstg[h2 * 64:(h2 + 1) * 64, :]),
                    inc=("dK", 16))
        else:
            ptv = pw1[:, 1024:1536]
            P.wait("tensor", "V", vs)
            P.wait("tensor", "S", t["vtpS"])
            P.wait("tensor", "S")
            nch = (r2w + 127) // 128
            for h2 in range(2):
                pb2 = pw1[:, 1024 + h2 * 512: 1536 + h2 * 512]
                for cchunk in range(nch):
                    w = min(128, r2w - cchunk * 128)
                    P.op("tensor", lambda e, rstg=rstg, h2=h2, cchunk=cchunk, w=w, pb2=pb2: e.matmul(
                        out=pb2[:, cchunk * 64: cchunk * 64 + 64][0:w, :],
                        lhsT=rstg[h2 * 64:(h2 + 1) * 64, cchunk * 128: cchunk * 128 + w],
                        rhs=t["ident"].ap()[h2 * 64:(h2 + 1) * 64, h2 * 64:(h2 + 1) * 64],
                        start=True, stop=True),
                        inc=("T", 1) if (h2 == 1 and cchunk == nch - 1) else None)
            tt = P.cur("T")
            if KR2S == 7:
                continue
            P.wait("scalar", "T", tt)
            P.wait("scalar", "dV")
            for h2 in range(2):
                pb2 = pw1[:, 1024 + h2 * 512: 1536 + h2 * 512]
                for cchunk in range(nch):
                    w = min(128, r2w - cchunk * 128)
                    P.op("scalar", lambda e, pb2=pb2, h2=h2, cchunk=cchunk, w=w, nch=nch: e.copy(
                        out=vtp[0:w, (h2 * nch + cchunk) * 64:(h2 * nch + cchunk) * 64 + 64],
                        in_=pb2[0:w, cchunk * 64: cchunk * 64 + 64]))
            t["vtpS"] = P.cur("S")
            ss = P.cur("S")
            if KR2S == 6:
                continue
            P.wait("gpsimd", "S", ss)
            for h2 in range(2):
                hh = 2 * pr + h2
                for cchunk in range(nch):
                    w = min(128, r2w - cchunk * 128)
                    P.op("gpsimd", lambda e, hh=hh, h2=h2, cchunk=cchunk, w=w, nch=nch: e.dma_start(
                        out=t["d_Vs"][hh, R2B + cchunk * 128: R2B + cchunk * 128 + w, :],
                        in_=vtp[0:w, (h2 * nch + cchunk) * 64:(h2 * nch + cchunk) * 64 + 64]),
                        inc=("dV", 16))
    P.op("gpsimd", lambda e: e.engine_nop())
    ar.release()




def ap_view(base_ap, extra_elem_off, dims, p0=0, np_=None):
    import dataclasses
    pitch, pcount = base_ap.ap[0]
    if np_ is None:
        np_ = pcount - p0
    return dataclasses.replace(
        base_ap, offset=base_ap.offset + extra_elem_off + p0 * pitch,
        ap=type(base_ap.ap)([[pitch, np_]] + [[s, c] for s, c in dims]))


def emit_attention(P, cfg, t):
    P.barrier()
    NB, NU, P1, Q0, Q1, G0, G1 = (cfg["NB"], cfg["NU"], cfg["P1"], cfg["Q0"],
                                  cfg["Q1"], cfg["G0"], cfg["G1"])
    nq = cfg["nq"]
    NG = G1 - G0 + 1
    ar = t["ar"]
    pw0, pw1 = t["pw0"], t["pw1"]
    ar.mark()

    ch1max = (60 * NB + 127) // 128
    ch2max = (19 * NU + 127) // 128
    nch = ch1max + ch2max
    NCOLK = 128 * nch

    Kq = ar.alloc(BF16, 4 * NCOLK)
    Vq = ar.alloc(BF16, 4 * nch * 65)
    qTq = ar.alloc(BF16, 4 * nq)
    vtailq = ar.alloc(BF16, 2 * nq)
    Abuf = ar.alloc(BF16, 3 * 1024)
    maskb = ar.alloc(BF16, 5 * 512)
    attq = ar.alloc(F32, nq, parts=4)
    aexpq = ar.alloc(F32, nq, parts=4)
    cq4 = ar.alloc(F32, nq, parts=4)
    ccols = ar.alloc(BF16, 16)
    mmb = ar.alloc(BF16, 28)
    cbias = ar.alloc(F32, 128)
    accS = ar.alloc(BF16, 2 * 544)
    vtts = ar.alloc(BF16, 512)
    atxs = ar.alloc(F32, 16)
    zs = ar.alloc(F32, 16)
    rzs = ar.alloc(F32, 16)
    t["qTq_dbg"] = qTq
    t["cq4_dbg"] = cq4
    t["Vq_dbg"] = Vq
    t["Vq_cols"] = 4 * nch * 65
    t["vtailq_dbg"], t["vtailq_cols"] = vtailq, 2 * nq
    if KDEBUG:
        zall = ar.alloc(F32, (nq // 128) * 4)
        atall = ar.alloc(F32, (nq // 128) * 4)
        t["zall_dbg"], t["atall_dbg"], t["zall_cols"] = zall, atall, (nq // 128) * 4
        vtall = ar.alloc(BF16, min(NG, 5) * 512)
        t["vtall_dbg"], t["vtall_cols"] = vtall, min(NG, 5) * 512
    ntq = nq // 128
    attn_out = ar.alloc(BF16, ntq * 1024)
    t["attn_out"] = attn_out
    t["attn_ntq"] = ntq
    if KDEBUG:
        P.op("vector", lambda e: e.memset(attn_out, -777.0))

    def lchunks(G):
        return (list(range(0, (240 * (G + 1) + 127) // 128)) +
                list(range(ch1max, ch1max + (19 * (G + 1) + 127) // 128)))

    def vones_col(c):
        return c if c < ch1max else 30 + (c - ch1max)

    def groups():
        for G in range(G0, G1 + 1):
            t0 = max(Q0 // 128, 4 * G)
            t1 = min(Q1 // 128, 4 * G + 4)
            yield G, t0, t1 - t0

    slotp = [pw0[:, 0:1024], pw0[:, 1024:2048], pw1[:, 0:1024]]
    accp = [pw1[:, 1024:1536], pw1[:, 1536:2048]]
    st_slotS = [0, 0, 0]
    st_slotV = [0, 0, 0]
    st_AbufT = [0, 0, 0]
    st_accS = [0, 0]
    rot = [0]

    for hq in range(4):
        heads = [4 * hq + j for j in range(4)]
        P.wait("gpsimd", "T")
        P.wait("gpsimd", "V")
        for hl in range(4):
            h = heads[hl]
            P.op("gpsimd", lambda e, h=h, hl=hl: e.dma_start(
                out=Kq[0:64, hl * NCOLK: hl * NCOLK + 128 * ch1max],
                in_=t["d_Ks"][h, :, 0:128 * ch1max]), inc=("dK", 16))
            P.op("gpsimd", lambda e, h=h, hl=hl: e.dma_start(
                out=Kq[0:64, hl * NCOLK + 128 * ch1max: (hl + 1) * NCOLK],
                in_=t["d_Ks"][h, :, R2B: R2B + 128 * ch2max]), inc=("dK", 16))
        if hq == 0:
            P.op("vector", lambda e: e.memset(Kq[64:65, :], 1.0))
        for j in range(4):
            h = heads[j]
            P.op("gpsimd", lambda e, h=h, j=j: e.dma_start(
                out=r3o(Vq[:, j * nch * 65:], 0, 64, ch1max, 65),
                in_=t["d_Vs"][h, 0:128 * ch1max, :].rearrange("(c p) d -> p c d", p=128)),
                inc=("dV", 16))
            P.op("gpsimd", lambda e, h=h, j=j: e.dma_start(
                out=r3o(Vq[:, (j * nch + ch1max) * 65:], 0, 64, ch2max, 65),
                in_=t["d_Vs"][h, R2B: R2B + 128 * ch2max, :].rearrange("(c p) d -> p c d", p=128)),
                inc=("dV", 16))
        for hl in range(4):
            h = heads[hl]
            P.op("gpsimd", lambda e, h=h, hl=hl: e.dma_start(
                out=qTq[0:64, hl * nq:(hl + 1) * nq], in_=t["d_qT"][64 * h:64 * (h + 1), 0:nq]),
                inc=("dQ", 16))
        for p2 in range(2):
            pr = 2 * hq + p2
            P.op("gpsimd", lambda e, pr=pr, p2=p2: e.dma_start(
                out=vtailq[:, p2 * nq:(p2 + 1) * nq], in_=t["d_vtailT"][128 * pr:128 * (pr + 1), 0:nq]),
                inc=("dVT", 16))
        P.wait("vector", "dV")
        for j in range(4):
            P.op("vector", lambda e, j=j: e.tensor_copy(
                out=r3o(Vq[:, j * nch * 65:], 64, 1, ch1max, 65), in_=t["vones"][:, 0:ch1max]))
            P.op("vector", lambda e, j=j: e.tensor_copy(
                out=r3o(Vq[:, (j * nch + ch1max) * 65:], 64, 1, ch2max, 65),
                in_=t["vones"][:, 30:30 + ch2max]))
        P.wait("tensor", "S", t["attS"])
        nqc = (nq + 511) // 512
        for z in range(nqc):
            w = min(512, nq - 512 * z)
            if z > 0:
                P.wait("tensor", "S")
            P.op("tensor", lambda e, z=z, w=w, hq=hq: e.matmul(
                out=accp[0][0:4, 0:w],
                lhsT=t["eind4"].ap()[:, 4 * hq: 4 * hq + 4],
                rhs=t["att_tail"].ap()[:, 512 * z: 512 * z + w],
                start=True, stop=True), inc=("T", 1))
            tsg = P.cur("T")
            P.wait("scalar", "T", tsg)
            if z == 0:
                P.wait("scalar", "V")
            P.op("scalar", lambda e, z=z, w=w: e.copy(
                out=attq[0:4, 512 * z: 512 * z + w], in_=accp[0][0:4, 0:w]))
        st_accS[0] = P.cur("S")

        if KSTOPA == 1:
            continue
        P.wait("tensor", "dK")
        P.wait("tensor", "dQ")
        for iG, (G, t0, nqt) in enumerate(groups()):
            nmini = 7 * (G + 1)
            P.wait("gpsimd", "V")
            P.op("gpsimd", lambda e, G=G: e.dma_start(
                out=mmb[:, :], in_=t["mmasks"][G]), inc=("dG", 16))
            mmld = P.cur("dG")
            for hl in range(4):
                sl = hl % 2
                bank = slotp[sl]
                P.wait("tensor", "S")
                P.wait("tensor", "V")
                for ti in range(nqt):
                    qcol = (t0 + ti) * 128 - Q0
                    P.op("tensor", lambda e, hl=hl, ti=ti, qcol=qcol, G=G, nmini=nmini, bank=bank: e.matmul(
                        out=bank[:, ti * 128: ti * 128 + nmini],
                        lhsT=qTq[0:64, hl * nq + qcol: hl * nq + qcol + 128],
                        rhs=ap_view(Kq, hl * NCOLK + 128 * ch1max + 12,
                                    [[19, G + 1], [1, 7]], p0=0, np_=64),
                        start=True, stop=True), inc=("T", 1))
                tsg = P.cur("T")
                P.wait("vector", "T")
                P.wait("vector", "dG", mmld)
                P.op("vector", lambda e, nqt=nqt, G=G, t0=t0, bank=bank: e.tensor_tensor(
                    out=ap_view(bank, 7 * G, [[128, nqt], [1, 7]]),
                    in0=ap_view(bank, 7 * G, [[128, nqt], [1, 7]]),
                    in1=ap_view(mmb, (t0 - 4 * G) * 7, [[7, nqt], [1, 7]]),
                    op=ALU.mult))
                P.op("vector", lambda e, hl=hl, nqt=nqt, nmini=nmini, bank=bank: e.tensor_reduce(
                    out=ccols[:, 0:nqt],
                    in_=ap_view(bank, 0, [[128, nqt], [1, nmini]]),
                    axis=AX.X, op=ALU.max))
                P.op("vector", lambda e, nqt=nqt: e.tensor_scalar(
                    out=ccols[:, 8:8 + nqt], in0=ccols[:, 0:nqt],
                    scalar1=-1.0, scalar2=80.0, op0=ALU.mult, op1=ALU.add))
                vsg = P.cur("V")
                P.wait("tensor", "V", vsg)
                P.wait("tensor", "S", st_accS[0])
                P.op("tensor", lambda e, nqt=nqt: e.matmul(
                    out=accp[0][0:nqt, 0:128], lhsT=ccols[:, 8:8 + nqt],
                    rhs=t["ident"].ap()[:, :], start=True, stop=True), inc=("T", 1))
                tsg = P.cur("T")
                P.wait("scalar", "T", tsg)
                P.wait("scalar", "dC")
                P.op("scalar", lambda e, nqt=nqt: e.copy(
                    out=cbias[0:nqt, 0:128], in_=accp[0][0:nqt, 0:128]))
                st_accS[0] = P.cur("S")
                crow_S = P.cur("S")
                P.wait("gpsimd", "S", crow_S)
                qcol0 = t0 * 128 - Q0
                P.op("gpsimd", lambda e, hl=hl, nqt=nqt, qcol0=qcol0: e.dma_start(
                    out=qTq[64:65, hl * nq + qcol0: hl * nq + qcol0 + nqt * 128],
                    in_=cbias[0:nqt, 0:128]), inc=("dC", 16), dma=True)
                P.op("gpsimd", lambda e, hl=hl, nqt=nqt, qcol0=qcol0: e.dma_start(
                    out=cq4[hl:hl + 1, qcol0: qcol0 + nqt * 128],
                    in_=cbias[0:nqt, 0:128]), inc=("dC", 16), dma=True)
        P.wait("tensor", "dC")
        P.wait("scalar", "dC")
        P.wait("vector", "dC")
        P.wait("vector", "dVT")
        P.wait("vector", "dQ")
        if KSTOPA == 2:
            continue

        for iG, (G, t0, nqt) in enumerate(groups()):
            NQ = nqt * 128
            qoff = t0 * 128 - Q0
            moff = t0 * 128 - 512 * G
            chl = lchunks(G)
            P.wait("gpsimd", "V")
            for sm in range(5):
                P.op("gpsimd", lambda e, G=G, sm=sm: e.dma_start(
                    out=maskb[:, sm * 512:(sm + 1) * 512], in_=t["masks"][G, sm]), inc=("dG", 16))
            mload = P.cur("dG")
            P.wait("vector", "S")
            P.op("vector", lambda e, qoff=qoff, NQ=NQ: e.tensor_tensor(
                out=aexpq[0:4, qoff:qoff + NQ], in0=attq[0:4, qoff:qoff + NQ],
                in1=cq4[0:4, qoff:qoff + NQ], op=ALU.add))
            P.wait("scalar", "V")
            P.op("scalar", lambda e, qoff=qoff, NQ=NQ: e.activation(
                out=aexpq[0:4, qoff:qoff + NQ], in_=aexpq[0:4, qoff:qoff + NQ],
                func=AF.Exp, scale=0.125))
            aexp_S = P.cur("S")
            P.wait("tensor", "S")
            P.wait("tensor", "V")
            for ti in range(nqt):
                P.op("tensor", lambda e, ti=ti, qoff=qoff: e.matmul(
                    out=slotp[2][0:128, ti * 4: ti * 4 + 4],
                    lhsT=aexpq[0:4, qoff + 128 * ti: qoff + 128 * (ti + 1)],
                    rhs=t["identf32"].ap()[0:4, 0:4], start=True, stop=True),
                    inc=("T", 1) if ti == nqt - 1 else None)
            tsg = P.cur("T")
            P.wait("scalar", "T", tsg)
            P.wait("scalar", "V")
            P.op("scalar", lambda e, nqt=nqt: e.copy(out=atxs[:, 0:4 * nqt], in_=slotp[2][:, 0:4 * nqt]))
            st_slotS[2] = P.cur("S")
            if KDEBUG:
                P.wait("vector", "S")
                P.op("vector", lambda e, nqt=nqt, t0=t0: e.tensor_copy(
                    out=t["atall_dbg"][:, (t0 - Q0 // 128) * 4:(t0 - Q0 // 128) * 4 + 4 * nqt],
                    in_=atxs[:, 0:4 * nqt]))

            mask_map = {}
            for k, gchunk in enumerate(MASK_META[G]):
                lc = gchunk if gchunk < 30 else ch1max + (gchunk - 30)
                mask_map[lc] = k

            for p2 in range(2):
                nb = (len(chl) + 1) // 2
                for h2 in range(2):
                    pass
                first = {0: True, 1: True}
                for cb in range(nb):
                    cs = chl[2 * cb: 2 * cb + 2]
                    for h2 in range(2):
                        hl = 2 * p2 + h2
                        sl = rot[0] % 3
                        rot[0] += 1
                        P.wait("tensor", "S")
                        P.wait("tensor", "V")
                        hl_ = 2 * p2 + h2
                        for j, c in enumerate(cs):
                            P.op("tensor", lambda e, hl_=hl_, c=c, j=j, sl=sl, qoff=qoff, NQ=NQ: e.matmul(
                                out=slotp[sl][:, j * 512: j * 512 + NQ],
                                lhsT=Kq[0:65, hl_ * NCOLK + 128 * c: hl_ * NCOLK + 128 * (c + 1)],
                                rhs=qTq[0:65, hl_ * nq + qoff: hl_ * nq + qoff + NQ],
                                start=True, stop=True), inc=("T", 1))
                        tsg = P.cur("T")
                        nvis = 0
                        for j, c in enumerate(cs):
                            if c in mask_map:
                                k = mask_map[c]
                                P.wait("vector", "T", tsg)
                                P.wait("vector", "dG", mload)
                                P.op("vector", lambda e, sl=sl, j=j, k=k, NQ=NQ, moff=moff: e.tensor_tensor(
                                    out=slotp[sl][:, j * 512: j * 512 + NQ],
                                    in0=slotp[sl][:, j * 512: j * 512 + NQ],
                                    in1=maskb[:, k * 512 + moff: k * 512 + moff + NQ], op=ALU.add))
                                nvis += 1
                        mask_V = P.cur("V")
                        P.wait("scalar", "T", tsg)
                        if nvis:
                            P.wait("scalar", "V", mask_V)
                        P.wait("scalar", "T", st_AbufT[sl])
                        P.op("scalar", lambda e, sl=sl, NQ=NQ, n=len(cs): e.activation(
                            out=ap_view(Abuf, sl * 1024, [[512, n], [1, NQ]]),
                            in_=ap_view(slotp[sl], 0, [[512, n], [1, NQ]]),
                            func=AF.Exp, scale=0.125))
                        st_slotS[sl] = P.cur("S")
                        exp_S = P.cur("S")
                        if KSTOPA == 3:
                            continue
                        P.wait("tensor", "S", exp_S)
                        if first[h2]:
                            P.wait("tensor", "S")
                        for j, c in enumerate(cs):
                            P.op("tensor", lambda e, hl=hl, h2=h2, c=c, j=j, sl=sl, NQ=NQ, st=first[h2] and j == 0: e.matmul(
                                out=accp[h2][0:65, 0:NQ],
                                lhsT=Vq[:, (hl * nch + c) * 65: (hl * nch + c) * 65 + 65],
                                rhs=Abuf[:, sl * 1024 + j * 512: sl * 1024 + j * 512 + NQ],
                                start=st, stop=False, skip_group_check=True), inc=("T", 1))
                        first[h2] = False
                        st_AbufT[sl] = P.cur("T")
                av_T = P.cur("T")
                if KSTOPA in (3, 4):
                    continue
                P.wait("scalar", "T", av_T)
                for h2 in range(2):
                    P.op("scalar", lambda e, h2=h2, NQ=NQ: e.copy(
                        out=accS[0:65, h2 * 544: h2 * 544 + NQ], in_=accp[h2][0:65, 0:NQ]))
                    st_accS[h2] = P.cur("S")
                ev_S = P.cur("S")
                if KSTOPA == 5:
                    continue
                slv = rot[0] % 3
                rot[0] += 1
                sla = rot[0] % 3
                rot[0] += 1
                P.wait("tensor", "S")
                P.wait("tensor", "V")
                for ti in range(nqt):
                    for h2 in range(2):
                        P.op("tensor", lambda e, ti=ti, h2=h2, p2=p2, qoff=qoff, slv=slv: e.matmul(
                            out=slotp[slv][:, h2 * 512 + ti * 64: h2 * 512 + ti * 64 + 64],
                            lhsT=vtailq[64 * h2:64 * (h2 + 1), p2 * nq + qoff + 128 * ti: p2 * nq + qoff + 128 * (ti + 1)],
                            rhs=t["ident"].ap()[64 * h2:64 * (h2 + 1), 64 * h2:64 * (h2 + 1)],
                            start=True, stop=True), inc=("T", 1))
                        P.op("tensor", lambda e, ti=ti, h2=h2, qoff=qoff, sla=sla: e.matmul(
                            out=slotp[sla][:, h2 * 512 + ti * 65: h2 * 512 + ti * 65 + 65],
                            lhsT=accS[0:65, h2 * 544 + 128 * ti: h2 * 544 + 128 * (ti + 1)],
                            rhs=t["ident"].ap()[0:65, 0:65], start=True, stop=True), inc=("T", 1))
                tp_T = P.cur("T")
                if KSTOPA == 6:
                    continue
                P.wait("scalar", "T", tp_T)
                P.wait("scalar", "V")
                P.op("scalar", lambda e, nqt=nqt, slv=slv: e.copy(
                    out=ap_view(vtts, 0, [[256, 2], [64, nqt], [1, 64]]),
                    in_=ap_view(slotp[slv], 0, [[512, 2], [64, nqt], [1, 64]])))
                if KDEBUG and p2 == 0 and iG < 5:
                    P.wait("vector", "S")
                    P.op("vector", lambda e, iG=iG: e.tensor_copy(
                        out=t["vtall_dbg"][:, iG * 512:(iG + 1) * 512], in_=vtts[:, 0:512]))
                st_slotS[slv] = P.cur("S")
                vt_S = P.cur("S")
                P.wait("vector", "T", tp_T)
                P.wait("vector", "S", vt_S)
                P.op("vector", lambda e, nqt=nqt, p2=p2, sla=sla: e.tensor_tensor(
                    out=ap_view(zs, 0, [[2, nqt], [1, 2]]),
                    in0=ap_view(slotp[sla], 64, [[65, nqt], [512, 2]]),
                    in1=ap_view(atxs, 2 * p2, [[4, nqt], [1, 2]]), op=ALU.add))
                P.op("vector", lambda e, nqt=nqt: e.tensor_scalar_add(
                    out=zs[:, 0:2 * nqt], in0=zs[:, 0:2 * nqt], scalar1=1e-30))
                if KDEBUG:
                    P.op("vector", lambda e, nqt=nqt, t0=t0, p2=p2: e.tensor_copy(
                        out=ap_view(t["zall_dbg"], (t0 - Q0 // 128) * 4 + 2 * p2, [[4, nqt], [1, 2]]),
                        in_=ap_view(zs, 0, [[2, nqt], [1, 2]])))
                P.op("vector", lambda e, nqt=nqt: e.reciprocal(out=rzs[:, 0:2 * nqt], in_=zs[:, 0:2 * nqt]))
                if KSTOPA == 7:
                    continue
                P.op("vector", lambda e, nqt=nqt, p2=p2: e.tensor_tensor(
                    out=ap_view(vtts, 0, [[256, 2], [64, nqt], [1, 64]]),
                    in0=ap_view(vtts, 0, [[256, 2], [64, nqt], [1, 64]]),
                    in1=ap_view(atxs, 2 * p2, [[1, 2], [4, nqt], [0, 64]]), op=ALU.mult))
                if KSTOPA == 8:
                    continue
                P.op("vector", lambda e, nqt=nqt, sla=sla: e.tensor_tensor(
                    out=ap_view(vtts, 0, [[256, 2], [64, nqt], [1, 64]]),
                    in0=ap_view(vtts, 0, [[256, 2], [64, nqt], [1, 64]]),
                    in1=ap_view(slotp[sla], 0, [[512, 2], [65, nqt], [1, 64]]), op=ALU.add))
                if KSTOPA == 9:
                    continue
                tile0_l = t0 - Q0 // 128
                P.op("vector", lambda e, nqt=nqt, p2=p2, tile0_l=tile0_l, hq=hq: e.tensor_tensor(
                    out=ap_view(attn_out, tile0_l * 1024 + 256 * hq + 128 * p2, [[64, 2], [1024, nqt], [1, 64]]),
                    in0=ap_view(vtts, 0, [[256, 2], [64, nqt], [1, 64]]),
                    in1=ap_view(rzs, 0, [[1, 2], [2, nqt], [0, 64]]), op=ALU.mult))
                st_slotV[sla] = P.cur("V")
                st_slotV[slv] = P.cur("V")
    P.wait("gpsimd", "V")
    for ti in range(ntq):
        P.op("gpsimd", lambda e, ti=ti: e.dma_start(
            out=t["d_ao"][ti * 128:(ti + 1) * 128, :],
            in_=attn_out[:, ti * 1024:(ti + 1) * 1024]), inc=("dQ", 16))
    P.op("gpsimd", lambda e: e.engine_nop())
    if not KDEBUG:
        ar.release()


def emit_ffn(P, cfg, t):
    P.barrier()
    nq, Q0, T0, T1 = cfg["nq"], cfg["Q0"], cfg["T0"], cfg["T1"]
    ntq = nq // 128
    T0c, T1c = T0 - Q0, T1 - Q0
    ar = t["ar"]
    pw0, pw1 = t["pw0"], t["pw1"]
    atail = t["att_tail"].ap()

    ar.mark()
    scal = ar.alloc(F32, 8)
    onesc = ar.alloc(F32, 1)
    onesp = ar.alloc(F32, 128, parts=1)
    epsb = ar.alloc(F32, 1)
    P.op("vector", lambda e: e.memset(onesc, 1.0))
    P.op("vector", lambda e: e.memset(onesp, 1.0))
    P.op("vector", lambda e: e.memset(epsb, 1e-5))

    ar.mark()
    wpb = ar.alloc(BF16, 8 * 1024)
    aob = ar.alloc(BF16, 2 * 1024)
    xrb = ar.alloc(F32, 2 * 1024)
    aoTs = ar.alloc(BF16, 1024)
    x1b = ar.alloc(F32, 1024)
    zf = ar.alloc(F32, 1024)
    rb = ar.alloc(BF16, 1024)
    abf = ar.alloc(BF16, 1024)
    bbf = ar.alloc(BF16, 1024)
    xqtb = ar.alloc(BF16, 1024)
    st = ar.alloc(F32, 16)
    for c in range(8):
        P.op("gpsimd", lambda e, c=c: e.dma_start(
            out=wpb[:, c * 1024:(c + 1) * 1024],
            in_=t["wprojT"][128 * c:128 * (c + 1), :]), inc=("dW", 16))
    for ti in range(ntq):
        pr = ti % 2
        P.wait("gpsimd", "T")
        P.wait("gpsimd", "V")
        P.op("gpsimd", lambda e, ti=ti, pr=pr: e.dma_start(
            out=aob[:, pr * 1024:(pr + 1) * 1024],
            in_=t["d_ao"][ti * 128:(ti + 1) * 128, :]), inc=("dK", 16))
        P.op("gpsimd", lambda e, ti=ti, pr=pr: e.dma_start(
            out=xrb[:, pr * 1024:(pr + 1) * 1024],
            in_=t["xres"][ti * 128:(ti + 1) * 128, :]), inc=("dV", 16))
        dk = P.cur("dK")
        P.wait("tensor", "dK", dk)
        if ti == 0:
            P.wait("tensor", "dW")
        P.wait("tensor", "S")
        P.wait("tensor", "V")
        for cb in range(8):
            P.op("tensor", lambda e, cb=cb, pr=pr: e.matmul(
                out=pw0[:, cb * 128:(cb + 1) * 128],
                lhsT=aob[:, pr * 1024 + cb * 128: pr * 1024 + (cb + 1) * 128],
                rhs=t["ident"].ap()[:, :], start=True, stop=True), inc=("T", 1))
        tT = P.cur("T")
        P.wait("scalar", "T", tT)
        P.op("scalar", lambda e: e.copy(out=aoTs[:, :], in_=pw0[:, 0:1024]))
        sS = P.cur("S")
        P.wait("tensor", "S", sS)
        for half in range(2):
            for cb in range(8):
                P.op("tensor", lambda e, half=half, cb=cb: e.matmul(
                    out=pw1[:, half * 512:(half + 1) * 512],
                    lhsT=aoTs[:, cb * 128:(cb + 1) * 128],
                    rhs=wpb[:, cb * 1024 + half * 512: cb * 1024 + half * 512 + 512],
                    start=(cb == 0), stop=(cb == 7)), inc=("T", 1))
        tT = P.cur("T")
        P.wait("vector", "T", tT)
        P.wait("vector", "dV")
        P.wait("vector", "dX1")
        P.wait("vector", "dZ")
        P.op("vector", lambda e, pr=pr: e.tensor_tensor(
            out=x1b[:, :], in0=pw1[:, 0:1024], in1=xrb[:, pr * 1024:(pr + 1) * 1024],
            op=ALU.add))
        P.op("vector", lambda e: e.tensor_reduce(
            out=st[:, 0:1], in_=x1b[:, :], axis=AX.X, op=ALU.add))
        vV = P.cur("V")
        P.wait("scalar", "V", vV)
        P.op("scalar", lambda e: e.activation(
            out=zf[:, :], in_=x1b[:, :], func=AF.Square, accum_out=st[:, 1:2]))
        sS = P.cur("S")
        P.wait("vector", "S", sS)
        P.op("vector", lambda e: e.tensor_scalar_mul(
            out=st[:, 2:3], in0=st[:, 0:1], scalar1=1.0 / 1024.0))
        P.op("vector", lambda e: e.tensor_scalar_mul(
            out=st[:, 3:4], in0=st[:, 1:2], scalar1=1.0 / 1024.0))
        P.op("vector", lambda e: e.tensor_tensor(
            out=st[:, 4:5], in0=st[:, 2:3], in1=st[:, 2:3], op=ALU.mult))
        P.op("vector", lambda e: e.tensor_tensor(
            out=st[:, 4:5], in0=st[:, 3:4], in1=st[:, 4:5], op=ALU.subtract))
        P.op("vector", lambda e: e.tensor_scalar_add(
            out=st[:, 4:5], in0=st[:, 4:5], scalar1=1e-5))
        vV = P.cur("V")
        P.wait("scalar", "V", vV)
        P.op("scalar", lambda e: e.activation(
            out=st[:, 5:6], in_=st[:, 4:5], func=AF.Sqrt))
        sS = P.cur("S")
        P.wait("vector", "S", sS)
        P.op("vector", lambda e: e.reciprocal(out=st[:, 5:6], in_=st[:, 5:6]))
        P.op("vector", lambda e: e.tensor_tensor(
            out=st[:, 6:7], in0=st[:, 2:3], in1=st[:, 5:6], op=ALU.mult))
        P.op("vector", lambda e: e.tensor_scalar_mul(
            out=st[:, 6:7], in0=st[:, 6:7], scalar1=-1.0))
        P.op("vector", lambda e: e.tensor_scalar(
            out=x1b[:, :], in0=x1b[:, :], scalar1=st[:, 5:6], scalar2=st[:, 6:7],
            op0=ALU.mult, op1=ALU.add))
        vV = P.cur("V")
        P.wait("gpsimd", "V", vV)
        P.op("gpsimd", lambda e, ti=ti: e.dma_start(
            out=t["d_r1"][ti * 128:(ti + 1) * 128, :], in_=x1b[:, :]), inc=("dX1", 16))
        P.op("vector", lambda e: e.tensor_reduce(
            out=st[:, 7:8], in_=x1b[:, :], axis=AX.X, op=ALU.max))
        P.op("vector", lambda e: e.tensor_reduce(
            out=st[:, 8:9], in_=x1b[:, :], axis=AX.X, op=ALU.min))
        P.op("vector", lambda e: e.tensor_scalar_mul(
            out=st[:, 8:9], in0=st[:, 8:9], scalar1=-1.0))
        P.op("vector", lambda e: e.tensor_tensor(
            out=st[:, 7:8], in0=st[:, 7:8], in1=st[:, 8:9], op=ALU.max))
        P.op("vector", lambda e: e.tensor_scalar_max(
            out=st[:, 7:8], in0=st[:, 7:8], scalar1=1e-5))
        P.op("vector", lambda e: e.reciprocal(out=st[:, 9:10], in_=st[:, 7:8]))
        vV = P.cur("V")
        P.wait("gpsimd", "V", vV)
        P.op("gpsimd", lambda e, ti=ti: e.dma_start(
            out=atail[0:1, ti * 128:(ti + 1) * 128], in_=st[:, 7:8]), inc=("dZ", 16))
        P.op("vector", lambda e: e.tensor_scalar_mul(
            out=st[:, 9:10], in0=st[:, 9:10], scalar1=127.0))
        P.op("vector", lambda e: e.tensor_scalar_mul(
            out=zf[:, :], in0=x1b[:, :], scalar1=st[:, 9:10]))
        P.op("vector", lambda e: e.tensor_scalar(
            out=abf[:, :], in0=zf[:, :], scalar1=0.0, scalar2=128.0, op0=ALU.max, op1=ALU.add))
        P.op("vector", lambda e: e.tensor_scalar(
            out=bbf[:, :], in0=zf[:, :], scalar1=0.0, scalar2=-128.0, op0=ALU.min, op1=ALU.add))
        P.op("vector", lambda e: e.tensor_tensor(
            out=rb[:, :], in0=abf[:, :], in1=bbf[:, :], op=ALU.add))
        vV = P.cur("V")
        P.wait("tensor", "V", vV)
        P.wait("tensor", "S")
        for cb in range(8):
            P.op("tensor", lambda e, cb=cb: e.matmul(
                out=pw1[:, 1024 + cb * 128: 1024 + (cb + 1) * 128],
                lhsT=rb[:, cb * 128:(cb + 1) * 128],
                rhs=t["ident"].ap()[:, :], start=True, stop=True), inc=("T", 1))
        tT = P.cur("T")
        P.wait("scalar", "T", tT)
        P.wait("scalar", "dQ")
        P.op("scalar", lambda e: e.copy(out=xqtb[:, :], in_=pw1[:, 1024:2048]))
        sS = P.cur("S")
        P.wait("gpsimd", "S", sS)
        for cb in range(8):
            P.op("gpsimd", lambda e, cb=cb, ti=ti: e.dma_start(
                out=t["d_xq"][cb * 128:(cb + 1) * 128, ti * 128:(ti + 1) * 128],
                in_=xqtb[:, cb * 128:(cb + 1) * 128]), inc=("dQ", 16))
    ar.release()
    P.barrier()

    ar.mark()
    wb16 = ar.alloc(BF16, 2 * 4096)
    wdmp = ar.alloc(BF16, 4096)
    macc = ar.alloc(F32, 4)
    wf32 = ar.alloc(F32, 2 * 4096)
    zf0 = ar.alloc(F32, 4096)
    tA = ar.alloc(BF16, 4096)
    tB = ar.alloc(BF16, 4096)
    tT0 = ar.alloc(BF16, 4096)
    P.op("vector", lambda e: e.memset(macc[:, 0:2], 0.0))

    def absmean_pass(src_ap, nchunks, ncols, mcol):
        for c in range(nchunks):
            pr = c % 2
            P.wait("gpsimd", "S")
            P.op("gpsimd", lambda e, c=c, pr=pr: e.dma_start(
                out=wb16[:, pr * 4096: pr * 4096 + ncols],
                in_=src_ap[128 * c:128 * (c + 1), :]), inc=("dW", 16))
            P.wait("scalar", "dW")
            P.wait("scalar", "V")
            P.op("scalar", lambda e, pr=pr, ncols=ncols: e.activation(
                out=wdmp[:, 0:ncols], in_=wb16[:, pr * 4096: pr * 4096 + ncols],
                func=AF.Abs, accum_out=macc[:, 2:3]))
            sS = P.cur("S")
            P.wait("vector", "S", sS)
            P.op("vector", lambda e, mcol=mcol: e.tensor_tensor(
                out=macc[:, mcol:mcol + 1], in0=macc[:, mcol:mcol + 1],
                in1=macc[:, 2:3], op=ALU.add))

    absmean_pass(t["w1T"], 8, 4096, 0)
    absmean_pass(t["w2T"], 32, 1024, 1)
    P.wait("tensor", "V")
    P.wait("tensor", "S")
    for i in range(2):
        P.op("tensor", lambda e, i=i: e.matmul(
            out=pw0[0:1, i:i + 1], lhsT=macc[:, i:i + 1], rhs=onesc[:, :],
            start=True, stop=True), inc=("T", 1))
    tT = P.cur("T")
    P.wait("scalar", "T", tT)
    P.op("scalar", lambda e: e.copy(out=macc[0:1, 2:4], in_=pw0[0:1, 0:2]))
    sS = P.cur("S")
    P.wait("vector", "S", sS)
    P.op("vector", lambda e: e.reciprocal(out=macc[0:1, 0:2], in_=macc[0:1, 2:4]))
    P.op("vector", lambda e: e.tensor_scalar_mul(
        out=macc[0:1, 0:2], in0=macc[0:1, 0:2], scalar1=4096.0 * 1024.0))
    P.op("vector", lambda e: e.tensor_scalar_mul(
        out=macc[0:1, 2:4], in0=macc[0:1, 2:4], scalar1=1.0 / (4096.0 * 1024.0 * 127.0)))
    vV = P.cur("V")
    P.wait("tensor", "V", vV)
    for i, src in enumerate((0, 2, 1, 3)):
        P.op("tensor", lambda e, i=i, src=src: e.matmul(
            out=pw0[:, 128 + i:128 + i + 1], lhsT=onesp[:, :],
            rhs=macc[0:1, src:src + 1], start=True, stop=True), inc=("T", 1))
    tT = P.cur("T")
    P.wait("scalar", "T", tT)
    P.op("scalar", lambda e: e.copy(out=scal[:, 0:4], in_=pw0[:, 128:132]))
    P.wait("gpsimd", "S")

    def tern_pass(src_ap, dst_ap, nchunks, ncols, rmcol):
        for c in range(nchunks):
            pr = c % 2
            P.wait("gpsimd", "V")
            P.op("gpsimd", lambda e, c=c, pr=pr: e.dma_start(
                out=wf32[:, pr * 4096: pr * 4096 + ncols],
                in_=src_ap[128 * c:128 * (c + 1), :]), inc=("dW", 16))
            P.wait("vector", "dW")
            P.wait("vector", "S")
            P.wait("vector", "dS")
            P.op("vector", lambda e, pr=pr, ncols=ncols, rmcol=rmcol: e.tensor_scalar_mul(
                out=zf0[:, 0:ncols], in0=wf32[:, pr * 4096: pr * 4096 + ncols],
                scalar1=scal[:, rmcol:rmcol + 1]))
            P.op("vector", lambda e, ncols=ncols: e.tensor_scalar(
                out=tA[:, 0:ncols], in0=zf0[:, 0:ncols], scalar1=0.0, scalar2=128.0,
                op0=ALU.max, op1=ALU.add))
            P.op("vector", lambda e, ncols=ncols: e.tensor_scalar(
                out=tB[:, 0:ncols], in0=zf0[:, 0:ncols], scalar1=0.0, scalar2=-128.0,
                op0=ALU.min, op1=ALU.add))
            P.op("vector", lambda e, ncols=ncols: e.tensor_tensor(
                out=tT0[:, 0:ncols], in0=tA[:, 0:ncols], in1=tB[:, 0:ncols], op=ALU.add))
            P.op("vector", lambda e, ncols=ncols: e.tensor_scalar(
                out=tT0[:, 0:ncols], in0=tT0[:, 0:ncols], scalar1=1.0, scalar2=-1.0,
                op0=ALU.min, op1=ALU.max))
            vV = P.cur("V")
            P.wait("gpsimd", "V", vV)
            P.op("gpsimd", lambda e, c=c, ncols=ncols: e.dma_start(
                out=dst_ap[128 * c:128 * (c + 1), :], in_=tT0[:, 0:ncols]), inc=("dS", 16))

    tern_pass(t["w1T"], t["d_w1t"], 8, 4096, 0)
    tern_pass(t["w2T"], t["d_w2t"], 32, 1024, 2)
    ar.release()
    P.barrier()

    NS0 = 1024
    NT = T1c - T0c
    nsl = (NT + NS0 - 1) // NS0
    bounds = [T0c + min(i * NS0, NT) for i in range(nsl + 1)]

    ar.mark()
    NSmax = NS0
    xq1s = ar.alloc(BF16, 8 * (NSmax + 2))
    hsl = [ar.alloc(BF16, NSmax + 2) for _ in range(32)]
    cvb = ar.alloc(BF16, NSmax)
    mxb = ar.alloc(BF16, NSmax)
    mnb = ar.alloc(BF16, NSmax)
    zsf = ar.alloc(F32, NSmax)
    hqa = ar.alloc(BF16, NSmax)
    hqb = ar.alloc(BF16, NSmax)
    a1sl = ar.alloc(F32, NSmax + 2)
    s2sl = ar.alloc(F32, NSmax)
    b2sl = ar.alloc(F32, NSmax)
    yfm = ar.alloc(BF16, 8 * NSmax)
    rows3 = ar.alloc(F32, 2 * NSmax)
    w1tb = ar.alloc(BF16, 2 * 1024)
    w2tb = ar.alloc(BF16, 2 * 128)
    x1rl = ar.alloc(BF16, 2 * 1024)
    x2b = ar.alloc(F32, 1024)
    outb = ar.alloc(F32, 2 * 1024)
    st2 = ar.alloc(F32, 16)
    cwb = ar.alloc(F32, 96)
    cbb = ar.alloc(F32, 32)
    P.op("gpsimd", lambda e: e.dma_start(out=cwb[:, :], in_=t["convw"]), inc=("dW", 16))
    P.op("gpsimd", lambda e: e.dma_start(out=cbb[:, :], in_=t["convb"]), inc=("dW", 16))
    P.wait("gpsimd", "dX1")
    P.wait("gpsimd", "dQ")
    P.wait("gpsimd", "dS")
    P.wait("scalar", "dW")
    P.wait("vector", "dW")

    for isl in range(nsl):
        s0, s1 = bounds[isl], bounds[isl + 1]
        NS = s1 - s0
        nb = NS // 128
        k0 = 1 if s0 - 1 < 0 else 0
        k1 = (min(nq, s1 + 1)) - (s0 - 1)
        P.wait("gpsimd", "T")
        for ci in range(8):
            P.op("gpsimd", lambda e, ci=ci, s0=s0, k0=k0, k1=k1, NS=NS: e.dma_start(
                out=xq1s[:, ci * (NS + 2) + k0: ci * (NS + 2) + k1],
                in_=t["d_xq"][128 * ci:128 * (ci + 1), s0 - 1 + k0: s0 - 1 + k1]),
                inc=("dQ", 16))
        P.wait("tensor", "S")
        P.wait("tensor", "V")
        for gb in range(0, k1, 512):
            g0, g1 = max(gb, k0), min(k1, gb + 512)
            if g0 >= g1:
                continue
            P.op("tensor", lambda e, g0=g0, g1=g1, s0=s0: e.matmul(
                out=pw1[:, g0:g1], lhsT=onesp[:, :],
                rhs=atail[0:1, s0 - 1 + g0: s0 - 1 + g1], start=True, stop=True),
                inc=("T", 1))
        tT = P.cur("T")
        P.wait("vector", "T", tT)
        P.op("vector", lambda e, k0=k0, k1=k1: e.tensor_scalar_mul(
            out=a1sl[:, k0:k1], in0=pw1[:, k0:k1], scalar1=scal[:, 1:2]))
        P.op("vector", lambda e, NS=NS: e.memset(mxb[:, 0:NS], -1e30))
        P.op("vector", lambda e, NS=NS: e.memset(mnb[:, 0:NS], 1e30))

        def w1load(c, prc):
            for ci in range(8):
                P.op("gpsimd", lambda e, ci=ci, c=c, prc=prc: e.dma_start(
                    out=w1tb[:, prc * 1024 + ci * 128: prc * 1024 + (ci + 1) * 128],
                    in_=t["d_w1t"][128 * ci:128 * (ci + 1), 128 * c:128 * (c + 1)]),
                    inc=("dK", 16))
        P.wait("gpsimd", "T")
        w1load(0, 0)
        dks = {0: P.cur("dK")}
        tslotw1 = {}
        evacV = None
        for c in range(32):
            prc = c % 2
            if c + 1 < 32:
                if (c + 1) % 2 in tslotw1:
                    P.wait("gpsimd", "T", tslotw1[(c + 1) % 2])
                w1load(c + 1, (c + 1) % 2)
                dks[c + 1] = P.cur("dK")
            P.wait("tensor", "dK", dks[c])
            if c == 0:
                P.wait("tensor", "dQ")
                P.wait("tensor", "S")
                P.wait("tensor", "V")
            else:
                P.wait("tensor", "V", evacV)
            for gb in range(0, k1, 512):
                g0, g1 = max(gb, k0), min(k1, gb + 512)
                if g0 >= g1:
                    continue
                for ci in range(8):
                    P.op("tensor", lambda e, ci=ci, g0=g0, g1=g1, prc=prc, NS=NS: e.matmul(
                        out=pw0[:, g0:g1],
                        lhsT=w1tb[:, prc * 1024 + ci * 128: prc * 1024 + (ci + 1) * 128],
                        rhs=xq1s[:, ci * (NS + 2) + g0: ci * (NS + 2) + g1],
                        start=(ci == 0), stop=(ci == 7)), inc=("T", 1))
            tslotw1[prc] = P.cur("T")
            tT = P.cur("T")
            P.wait("vector", "T", tT)
            if k0 == 1:
                P.op("vector", lambda e, c=c: e.memset(hsl[c][:, 0:1], 0.0))
            if k1 < NS + 2:
                P.op("vector", lambda e, c=c, k1=k1, NS=NS: e.memset(hsl[c][:, k1:NS + 2], 0.0))
            P.op("vector", lambda e, c=c, k0=k0, k1=k1: e.tensor_tensor(
                out=hsl[c][:, k0:k1], in0=pw0[:, k0:k1], in1=a1sl[:, k0:k1], op=ALU.mult))
            evacV = P.cur("V")
            P.op("vector", lambda e, c=c, NS=NS: e.tensor_scalar_mul(
                out=cvb[:, 0:NS], in0=hsl[c][:, 2:NS + 2],
                scalar1=cwb[:, 3 * c + 2:3 * c + 3]))
            P.op("vector", lambda e, c=c, NS=NS: e.scalar_tensor_tensor(
                out=cvb[:, 0:NS], in0=hsl[c][:, 0:NS],
                scalar=cwb[:, 3 * c:3 * c + 1], in1=cvb[:, 0:NS],
                op0=ALU.mult, op1=ALU.add))
            P.op("vector", lambda e, c=c, NS=NS: e.scalar_tensor_tensor(
                out=cvb[:, 0:NS], in0=hsl[c][:, 1:NS + 1],
                scalar=cwb[:, 3 * c + 1:3 * c + 2], in1=cvb[:, 0:NS],
                op0=ALU.mult, op1=ALU.add))
            vV = P.cur("V")
            P.wait("scalar", "V", vV)
            P.op("scalar", lambda e, c=c, NS=NS: e.activation(
                out=hsl[c][:, 0:NS], in_=cvb[:, 0:NS], func=AF.Silu,
                bias=cbb[:, c:c + 1]))
            sS = P.cur("S")
            P.wait("vector", "S", sS)
            P.op("vector", lambda e, c=c, NS=NS: e.tensor_tensor(
                out=mxb[:, 0:NS], in0=mxb[:, 0:NS], in1=hsl[c][:, 0:NS], op=ALU.max))
            P.op("vector", lambda e, c=c, NS=NS: e.tensor_tensor(
                out=mnb[:, 0:NS], in0=mnb[:, 0:NS], in1=hsl[c][:, 0:NS], op=ALU.min))
        P.op("vector", lambda e, NS=NS: e.tensor_scalar_mul(
            out=mnb[:, 0:NS], in0=mnb[:, 0:NS], scalar1=-1.0))
        P.op("vector", lambda e, NS=NS: e.tensor_tensor(
            out=mxb[:, 0:NS], in0=mxb[:, 0:NS], in1=mnb[:, 0:NS], op=ALU.max))
        vV = P.cur("V")
        P.wait("tensor", "V", vV)
        P.wait("tensor", "S")
        for b in range(nb):
            P.op("tensor", lambda e, b=b: e.matmul(
                out=pw1[:, 1024 + b * 128: 1024 + (b + 1) * 128],
                lhsT=mxb[:, b * 128:(b + 1) * 128], rhs=t["ident"].ap()[:, :],
                start=True, stop=True), inc=("T", 1))
        tT = P.cur("T")
        P.wait("vector", "T", tT)
        P.wait("vector", "dZ")
        for b in range(nb):
            P.op("vector", lambda e, b=b: e.tensor_reduce(
                out=st2[:, b:b + 1], in_=pw1[:, 1024 + b * 128: 1024 + (b + 1) * 128],
                axis=AX.X, op=ALU.max))
        P.op("vector", lambda e, nb=nb: e.tensor_scalar_max(
            out=st2[:, 0:nb], in0=st2[:, 0:nb], scalar1=1e-5))
        P.op("vector", lambda e, nb=nb: e.reciprocal(
            out=st2[:, 8:8 + nb], in_=st2[:, 0:nb]))
        P.op("vector", lambda e, nb=nb: e.tensor_scalar_mul(
            out=st2[:, 8:8 + nb], in0=st2[:, 8:8 + nb], scalar1=127.0))
        P.op("vector", lambda e, nb=nb: e.tensor_scalar_mul(
            out=st2[:, 0:nb], in0=st2[:, 0:nb], scalar1=scal[:, 3:4]))
        vV = P.cur("V")
        P.wait("gpsimd", "V", vV)
        for b in range(nb):
            P.op("gpsimd", lambda e, b=b: e.dma_start(
                out=rows3[0:1, b * 128:(b + 1) * 128],
                in_=st2[:, 8 + b:8 + b + 1]), inc=("dZ", 16))
            P.op("gpsimd", lambda e, b=b, NSmax=NSmax: e.dma_start(
                out=rows3[0:1, NSmax + b * 128: NSmax + (b + 1) * 128],
                in_=st2[:, b:b + 1]), inc=("dZ", 16))
        P.wait("tensor", "dZ")
        P.wait("tensor", "V")
        P.wait("tensor", "S")
        for g0 in range(0, NS, 512):
            g1 = min(NS, g0 + 512)
            P.op("tensor", lambda e, g0=g0, g1=g1: e.matmul(
                out=pw1[:, g0:g1], lhsT=onesp[:, :],
                rhs=rows3[0:1, g0:g1], start=True, stop=True), inc=("T", 1))
            P.op("tensor", lambda e, g0=g0, g1=g1, NSmax=NSmax: e.matmul(
                out=pw1[:, 1024 + g0: 1024 + g1], lhsT=onesp[:, :],
                rhs=rows3[0:1, NSmax + g0: NSmax + g1], start=True, stop=True), inc=("T", 1))
        tT = P.cur("T")
        P.wait("vector", "T", tT)
        P.op("vector", lambda e, NS=NS: e.tensor_copy(out=s2sl[:, 0:NS], in_=pw1[:, 0:NS]))
        P.op("vector", lambda e, NS=NS: e.tensor_copy(out=b2sl[:, 0:NS], in_=pw1[:, 1024:1024 + NS]))
        for c in range(32):
            P.op("vector", lambda e, c=c, NS=NS: e.tensor_tensor(
                out=zsf[:, 0:NS], in0=hsl[c][:, 0:NS], in1=s2sl[:, 0:NS], op=ALU.mult))
            P.op("vector", lambda e, NS=NS: e.tensor_scalar(
                out=hqa[:, 0:NS], in0=zsf[:, 0:NS], scalar1=0.0, scalar2=128.0,
                op0=ALU.max, op1=ALU.add))
            P.op("vector", lambda e, NS=NS: e.tensor_scalar(
                out=hqb[:, 0:NS], in0=zsf[:, 0:NS], scalar1=0.0, scalar2=-128.0,
                op0=ALU.min, op1=ALU.add))
            P.op("vector", lambda e, c=c, NS=NS: e.tensor_tensor(
                out=hsl[c][:, 0:NS], in0=hqa[:, 0:NS], in1=hqb[:, 0:NS], op=ALU.add))
            P.op("vector", lambda e, c=c, NS=NS: e.tensor_scalar(
                out=hsl[c][:, 0:NS], in0=hsl[c][:, 0:NS], scalar1=127.0, scalar2=-128.0,
                op0=ALU.min, op1=ALU.max))
        def w2load(cob, c, prc):
            P.op("gpsimd", lambda e, c=c, cob=cob, prc=prc: e.dma_start(
                out=w2tb[:, prc * 128:(prc + 1) * 128],
                in_=t["d_w2t"][128 * c:128 * (c + 1), 128 * cob:128 * (cob + 1)]),
                inc=("dV", 16))
        for cob in range(8):
            P.wait("tensor", "S")
            P.wait("tensor", "V")
            P.wait("gpsimd", "T")
            w2load(cob, 0, 0)
            dvs = {0: P.cur("dV")}
            tslotw2 = {}
            for c in range(32):
                prc = c % 2
                if c + 1 < 32:
                    if (c + 1) % 2 in tslotw2:
                        P.wait("gpsimd", "T", tslotw2[(c + 1) % 2])
                    w2load(cob, c + 1, (c + 1) % 2)
                    dvs[c + 1] = P.cur("dV")
                P.wait("tensor", "dV", dvs[c])
                for g0 in range(0, NS, 512):
                    g1 = min(NS, g0 + 512)
                    P.op("tensor", lambda e, c=c, g0=g0, g1=g1, prc=prc: e.matmul(
                        out=pw1[:, g0:g1],
                        lhsT=w2tb[:, prc * 128:(prc + 1) * 128],
                        rhs=hsl[c][:, g0:g1],
                        start=(c == 0), stop=(c == 31)), inc=("T", 1))
                tslotw2[prc] = P.cur("T")
            tT = P.cur("T")
            P.wait("vector", "T", tT)
            P.op("vector", lambda e, cob=cob, NS=NS: e.tensor_tensor(
                out=yfm[:, cob * NS: cob * NS + NS], in0=pw1[:, 0:NS],
                in1=b2sl[:, 0:NS], op=ALU.mult))
        for b in range(nb):
            prb = b % 2
            P.wait("gpsimd", "V")
            P.op("gpsimd", lambda e, b=b, s0=s0, prb=prb: e.dma_start(
                out=x1rl[:, prb * 1024:(prb + 1) * 1024],
                in_=t["d_r1"][(s0 + b * 128):(s0 + (b + 1) * 128), :]), inc=("dX0", 16))
            dx = P.cur("dX0")
            P.wait("tensor", "V")
            P.wait("tensor", "S")
            for cob in range(8):
                P.op("tensor", lambda e, cob=cob, b=b, NS=NS: e.matmul(
                    out=pw0[:, cob * 128:(cob + 1) * 128],
                    lhsT=yfm[:, cob * NS + b * 128: cob * NS + (b + 1) * 128],
                    rhs=t["ident"].ap()[:, :], start=True, stop=True), inc=("T", 1))
            tT = P.cur("T")
            P.wait("vector", "T", tT)
            P.wait("vector", "dX0", dx)
            P.op("vector", lambda e, prb=prb: e.tensor_tensor(
                out=x2b[:, :], in0=pw0[:, 0:1024], in1=x1rl[:, prb * 1024:(prb + 1) * 1024],
                op=ALU.add))
            P.op("vector", lambda e: e.tensor_reduce(
                out=st2[:, 12:13], in_=x2b[:, :], axis=AX.X, op=ALU.add))
            vV = P.cur("V")
            P.wait("scalar", "V", vV)
            P.op("scalar", lambda e: e.activation(
                out=pw0[:, 1024:2048], in_=x2b[:, :], func=AF.Square, accum_out=st2[:, 13:14]))
            sS = P.cur("S")
            P.wait("vector", "S", sS)
            P.op("vector", lambda e: e.tensor_scalar_mul(
                out=st2[:, 12:13], in0=st2[:, 12:13], scalar1=1.0 / 1024.0))
            P.op("vector", lambda e: e.tensor_scalar_mul(
                out=st2[:, 13:14], in0=st2[:, 13:14], scalar1=1.0 / 1024.0))
            P.op("vector", lambda e: e.tensor_tensor(
                out=st2[:, 14:15], in0=st2[:, 12:13], in1=st2[:, 12:13], op=ALU.mult))
            P.op("vector", lambda e: e.tensor_tensor(
                out=st2[:, 14:15], in0=st2[:, 13:14], in1=st2[:, 14:15], op=ALU.subtract))
            P.op("vector", lambda e: e.tensor_scalar_add(
                out=st2[:, 14:15], in0=st2[:, 14:15], scalar1=1e-5))
            vV = P.cur("V")
            P.wait("scalar", "V", vV)
            P.op("scalar", lambda e: e.activation(
                out=st2[:, 15:16], in_=st2[:, 14:15], func=AF.Sqrt))
            sS = P.cur("S")
            P.wait("vector", "S", sS)
            P.op("vector", lambda e: e.reciprocal(out=st2[:, 15:16], in_=st2[:, 15:16]))
            P.op("vector", lambda e: e.tensor_tensor(
                out=st2[:, 14:15], in0=st2[:, 12:13], in1=st2[:, 15:16], op=ALU.mult))
            P.op("vector", lambda e: e.tensor_scalar_mul(
                out=st2[:, 14:15], in0=st2[:, 14:15], scalar1=-1.0))
            P.wait("vector", "dS")
            P.op("vector", lambda e, prb=prb: e.tensor_scalar(
                out=outb[:, prb * 1024:(prb + 1) * 1024], in0=x2b[:, :],
                scalar1=st2[:, 15:16], scalar2=st2[:, 14:15], op0=ALU.mult, op1=ALU.add))
            vV = P.cur("V")
            P.wait("gpsimd", "V", vV)
            P.op("gpsimd", lambda e, b=b, s0=s0, prb=prb: e.dma_start(
                out=t["out"][(s0 + b * 128):(s0 + (b + 1) * 128), :],
                in_=outb[:, prb * 1024:(prb + 1) * 1024]), inc=("dS", 16))
    P.wait("gpsimd", "dS")
    ar.release()
    ar.release()


def build_nc():
    nc = bass.Bass("TRN2", target_bir_lowering=False, debug=False)
    tt = {}
    tt["xb"] = nc.dram_tensor("xb", [T, C], F32, kind="ExternalInput").ap()
    tt["xres"] = nc.dram_tensor("xres", [NQMAX, C], F32, kind="ExternalInput").ap()
    tt["wattnT"] = nc.dram_tensor("wattnT", [C, 3 * C], F32, kind="ExternalInput").ap()
    tt["wprojT"] = nc.dram_tensor("wprojT", [C, C], F32, kind="ExternalInput").ap()
    tt["w1T"] = nc.dram_tensor("w1T", [C, 4096], F32, kind="ExternalInput").ap()
    tt["w2T"] = nc.dram_tensor("w2T", [4096, C], F32, kind="ExternalInput").ap()
    tt["convw"] = nc.dram_tensor("convw", [128, 96], F32, kind="ExternalInput").ap()
    tt["convb"] = nc.dram_tensor("convb", [128, 32], F32, kind="ExternalInput").ap()
    tt["identf"] = nc.dram_tensor("identf", [128, 128], F32, kind="ExternalInput").ap()
    tt["bonesf"] = nc.dram_tensor("bonesf", [128, 128], F32, kind="ExternalInput").ap()
    tt["masks"] = nc.dram_tensor("masks", [16, 5, 128, 512], BF16, kind="ExternalInput").ap()
    tt["mmasks"] = nc.dram_tensor("mmasks", [16, 128, 28], BF16, kind="ExternalInput").ap()
    tt["vonesd"] = nc.dram_tensor("vones", [128, 33], BF16, kind="ExternalInput").ap()
    tt["eind4f"] = nc.dram_tensor("eind4f", [16, 16], F32, kind="ExternalInput").ap()
    tt["eind44f"] = nc.dram_tensor("eind44f", [4, 512], F32, kind="ExternalInput").ap()
    tt["rank"] = nc.dram_tensor("rank", [1, 1], I32, kind="ExternalInput").ap()
    tt["out"] = nc.dram_tensor("out", [NQMAX, C], F32, kind="ExternalOutput").ap()

    tt["d_qT"] = nc.dram_tensor("d_qT", [C, NQMAX], BF16, kind="Internal").ap()
    d_Ks_h = nc.dram_tensor("d_Ks", [16, 64, SMAX], BF16, kind="Internal")
    d_Vs_h = nc.dram_tensor("d_Vs", [16, SMAX, 64], BF16, kind="Internal")
    tt["d_Ks"] = d_Ks_h.ap()
    tt["d_Vs"] = d_Vs_h.ap()
    tt["d_KsF"] = d_Ks_h.ap().flatten()
    tt["d_VsF"] = d_Vs_h.ap().flatten()
    tt["d_vtailT"] = nc.dram_tensor("d_vtailT", [C, NQMAX], BF16, kind="Internal").ap()
    if KSTOP == 0 or KSTOP >= 20:
        tt["d_ao"] = nc.dram_tensor("d_ao", [NQMAX, C], BF16, kind="Internal").ap()
    tt["d_xq"] = nc.dram_tensor("d_xq", [C, NQMAX], BF16, kind="Internal").ap()
    tt["d_r1"] = nc.dram_tensor("d_r1", [NQMAX, C], BF16, kind="Internal").ap()
    tt["d_w1t"] = nc.dram_tensor("d_w1t", [C, 4096], BF16, kind="Internal").ap()
    tt["d_w2t"] = nc.dram_tensor("d_w2t", [4096, C], BF16, kind="Internal").ap()

    if KDEBUG:
        tt["dbg_Ks"] = nc.dram_tensor("dbg_Ks", [16, 64, SMAX], F32, kind="ExternalOutput").ap()
        tt["dbg_Vs"] = nc.dram_tensor("dbg_Vs", [16, SMAX, 64], F32, kind="ExternalOutput").ap()
        tt["dbg_qT"] = nc.dram_tensor("dbg_qT", [C, NQMAX], F32, kind="ExternalOutput").ap()
        tt["dbg_vt"] = nc.dram_tensor("dbg_vt", [C, NQMAX], F32, kind="ExternalOutput").ap()
        tt["dbg_at"] = nc.dram_tensor("dbg_at", [16, NQMAX], F32, kind="ExternalOutput").ap()
        tt["dbg_ao"] = nc.dram_tensor("dbg_ao", [NQMAX // 128, 128, C], F32, kind="ExternalOutput").ap()
        tt["dbg_crow"] = nc.dram_tensor("dbg_crow", [1, 4 * NQMAX], F32, kind="ExternalOutput").ap()
        tt["dbg_cq4"] = nc.dram_tensor("dbg_cq4", [4, NQMAX], F32, kind="ExternalOutput").ap()
        tt["dbg_Vq"] = nc.dram_tensor("dbg_Vq", [128, 9216], F32, kind="ExternalOutput").ap()
        tt["dbg_zall"] = nc.dram_tensor("dbg_zall", [128, 128], F32, kind="ExternalOutput").ap()
        tt["dbg_atall"] = nc.dram_tensor("dbg_atall", [128, 128], F32, kind="ExternalOutput").ap()
        tt["dbg_vtall"] = nc.dram_tensor("dbg_vtall", [128, 8 * 512], F32, kind="ExternalOutput").ap()
        tt["dbg_vtq"] = nc.dram_tensor("dbg_vtq", [128, 2 * NQMAX], F32, kind="ExternalOutput").ap()

    from contextlib import ExitStack
    with ExitStack() as ctx:
        e = ctx.enter_context
        arena_h = e(nc.sbuf_tensor("arena", [128, 170 * 1024], mybir.dt.uint8))
        att_h = e(nc.sbuf_tensor("att_tail", [16, NQMAX], F32))
        ident_h = e(nc.sbuf_tensor("identb", [128, 128], BF16))
        bones_h = e(nc.sbuf_tensor("bonesb", [128, 128], BF16))
        kvb_h = e(nc.sbuf_tensor("kvb", [128, 1024], BF16))
        rank_h = e(nc.sbuf_tensor("rankb", [1, 1], I32))
        vones_h = e(nc.sbuf_tensor("vonesb", [128, 33], BF16))
        eind4_h = e(nc.sbuf_tensor("eind4b", [16, 16], F32))
        eind44_h = e(nc.sbuf_tensor("eind44b", [4, 512], F32))
        identf32_h = e(nc.sbuf_tensor("identf32b", [128, 128], F32))
        pw0_h = e(nc.psum_tensor("pw0", [128, 2048], F32))
        pw1_h = e(nc.psum_tensor("pw1", [128, 2048], F32))
        sems = {}
        for s in ("dI", "dG", "dS", "dW", "dX0", "dX1", "dK", "dV", "dVT", "dQ", "dZ", "dC", "T", "V", "S", "G"):
            sems[s] = e(nc.semaphore(s))
        block = e(nc.Block())

        plans = {}
        for s in KBODIES:
            tdict = dict(tt)
            tdict["ar"] = Arena(arena_h, 170 * 1024)
            tdict["att_tail"] = att_h
            tdict["ident"] = ident_h
            tdict["bones"] = bones_h
            tdict["kvb"] = kvb_h.ap()
            tdict["vones"] = vones_h.ap()
            tdict["eind4"] = eind4_h
            tdict["eind44"] = eind44_h
            tdict["identf32"] = identf32_h
            tdict["pw0"] = pw0_h.ap()
            tdict["pw1"] = pw1_h.ap()
            tdict["xfree"] = [0, 0]
            tdict["xtS"] = [0, 0]
            tdict["qkvV"] = [0, 0]
            tdict["qkvS"] = [0, 0]
            tdict["kvS"] = [0, 0]
            tdict["tailV"] = [0, 0]
            tdict["vtpS"] = 0
            tdict["attS"] = 0
            tdict["attT"] = 0
            tdict["prodV"] = 0
            tdict["qF"] = 0
            tdict["vt8F"] = 0
            tdict["vtpF"] = [0, 0]
            tdict["stgF"] = {(0, 0): 0, (0, 1): 0, (1, 0): 0, (1, 1): 0}
            Pl = Plan()
            cfg = slice_cfg(s)
            emit_qkv(Pl, cfg, tdict)
            if KSTOP == 0 or KSTOP >= 20:
                emit_attention(Pl, cfg, tdict)
            if KSTOP == 0:
                emit_ffn(Pl, cfg, tdict)
            if KDEBUG:
                emit_debug_dump(Pl, cfg, tdict)
            plans[s] = Pl

        @block.sync
        def _(sync):
            sync.dma_start(out=rank_h.ap()[:, :], in_=tt["rank"]).then_inc(sems["dI"], 16)
            for s in KBODIES:
                assert not plans[s].ops["sync"], "sync engine must stay branch-free"

        for name, dec in (("tensor", block.tensor), ("vector", block.vector),
                          ("scalar", block.scalar), ("gpsimd", block.gpsimd)):
            def f(eng, name=name):
                rr = eng.to_reg(0)
                eng.wait_ge(sems["dI"], 16)
                rr = eng.to_reg(0)
                eng.reg_load(rr, rank_h.ap()[0:1, 0:1])
                for s in KBODIES:
                    with eng.If_eq(rr, s):
                        plans[s].replay(name, eng, sems)
            dec(f)
    return nc


def emit_debug_dump(P, cfg, t):
    for sname in ("dK", "dV", "dVT", "dQ"):
        P.wait("gpsimd", sname)
    P.wait("gpsimd", "V")
    P.wait("gpsimd", "S")
    P.op("gpsimd", lambda e: e.dma_start(out=t["dbg_Ks"], in_=t["d_Ks"]), inc=("dG", 16))
    P.op("gpsimd", lambda e: e.dma_start(out=t["dbg_Vs"], in_=t["d_Vs"]), inc=("dG", 16))
    nq = cfg["nq"]
    P.op("gpsimd", lambda e, nq=nq: e.dma_start(out=t["dbg_qT"][:, 0:nq], in_=t["d_qT"][:, 0:nq]), inc=("dG", 16))
    P.op("gpsimd", lambda e, nq=nq: e.dma_start(out=t["dbg_vt"][:, 0:nq], in_=t["d_vtailT"][:, 0:nq]), inc=("dG", 16))
    P.op("gpsimd", lambda e, nq=nq: e.dma_start(out=t["dbg_at"][:, 0:nq], in_=t["att_tail"].ap()[:, 0:nq]), inc=("dG", 16))
    if "attn_out" in t:
        P.op("gpsimd", lambda e: e.dma_start(out=t["dbg_crow"][:, 0:4 * cfg["nq"]],
                                             in_=t["qTq_dbg"][64:65, :]), inc=("dG", 16))
        P.op("gpsimd", lambda e: e.dma_start(out=t["dbg_cq4"][:, 0:cfg["nq"]],
                                             in_=t["cq4_dbg"][0:4, :]), inc=("dG", 16))
        if "Vq_dbg" in t:
            P.op("gpsimd", lambda e: e.dma_start(out=t["dbg_Vq"][:, 0:t["Vq_cols"]],
                                                 in_=t["Vq_dbg"]), inc=("dG", 16))
        if "vtailq_dbg" in t:
            P.op("gpsimd", lambda e: e.dma_start(out=t["dbg_vtq"][:, 0:t["vtailq_cols"]],
                                                 in_=t["vtailq_dbg"]), inc=("dG", 16))
        if "zall_dbg" in t:
            P.op("gpsimd", lambda e: e.dma_start(out=t["dbg_zall"][:, 0:t["zall_cols"]],
                                                 in_=t["zall_dbg"]), inc=("dG", 16))
            P.op("gpsimd", lambda e: e.dma_start(out=t["dbg_atall"][:, 0:t["zall_cols"]],
                                                 in_=t["atall_dbg"]), inc=("dG", 16))
            P.op("gpsimd", lambda e: e.dma_start(out=t["dbg_vtall"][:, 0:t["vtall_cols"]],
                                                 in_=t["vtall_dbg"]), inc=("dG", 16))
        ao = t["attn_out"]
        for ti in range(t["attn_ntq"]):
            P.op("gpsimd", lambda e, ti=ti, ao=ao: e.dma_start(
                out=t["dbg_ao"][ti], in_=ao[:, ti * 1024:(ti + 1) * 1024]), inc=("dG", 16))
    P.op("gpsimd", lambda e: e.dma_start(out=t["out"], in_=t["xres"]), inc=("dG", 16))
    P.wait("gpsimd", "dG")


_NC_CACHE = {}


def _get_nc():
    key = (KDEBUG, tuple(KBODIES))
    if key not in _NC_CACHE:
        _NC_CACHE[key] = build_nc()
    return _NC_CACHE[key]


def make_inputs(L0_attn_x, L0_attn_W_attn, L0_attn_W_proj, L0_ffn_w1, L0_ffn_conv_w,
                L0_ffn_conv_b, L0_ffn_w2):
    x = np.ascontiguousarray(np.asarray(L0_attn_x, dtype=np.float32))
    bf = ml_dtypes.bfloat16
    wattnT = np.ascontiguousarray(np.asarray(L0_attn_W_attn, np.float32).T)
    wprojT = np.ascontiguousarray(np.asarray(L0_attn_W_proj, np.float32).T)
    w1T = np.ascontiguousarray(np.asarray(L0_ffn_w1, np.float32).T)
    w2T = np.ascontiguousarray(np.asarray(L0_ffn_w2, np.float32).T)
    cw = np.asarray(L0_ffn_conv_w, np.float32).reshape(4096, 3)
    convw = np.zeros((128, 96), np.float32)
    for cchunk in range(32):
        for j in range(3):
            convw[:, 3 * cchunk + j] = cw[128 * cchunk:128 * (cchunk + 1), j]
    cb = np.asarray(L0_ffn_conv_b, np.float32)
    convb = np.zeros((128, 32), np.float32)
    for cchunk in range(32):
        convb[:, cchunk] = cb[128 * cchunk:128 * (cchunk + 1)]
    identf = np.eye(128, dtype=np.float32)
    eind44 = np.zeros((4, 512), np.float32)
    for hl in range(4):
        eind44[hl, hl * 128:(hl + 1) * 128] = 1.0
    bonesf = np.zeros((128, 128), np.float32)
    for pr in range(8):
        bonesf[0:64, pr * 16 + 2 * pr] = 1.0
        bonesf[64:128, pr * 16 + 2 * pr + 1] = 1.0
    in_maps = []
    for c in range(8):
        b, s = c // 4, c % 4
        cfg = slice_cfg(s)
        xres = np.zeros((NQMAX, C), np.float32)
        xres[0:cfg["nq"]] = x[b, cfg["Q0"]:cfg["Q1"]]
        vones = np.zeros((128, 33), np.float32)
        r, real = set_r_index(cfg["NB"], cfg["NU"])
        for row in range(60 * cfg["NB"]):
            vones[row % 128, row // 128] = 1.0
        for u in range(cfg["NU"]):
            for j, L, i in r2_slots(u):
                row = R2B + 19 * u + j
                vones[row % 128, row // 128] = 1.0
        in_maps.append({
            "xb": x[b], "xres": xres, "wattnT": wattnT, "wprojT": wprojT,
            "w1T": w1T, "w2T": w2T, "convw": convw, "convb": convb,
            "identf": identf, "bonesf": bonesf, "masks": np.asarray(MASK_DATA),
            "mmasks": np.asarray(MMASK),
            "eind4f": np.eye(16, dtype=np.float32),
            "eind44f": eind44,
            "vones": vones.astype(bf), "rank": np.array([[s]], np.int32),
        })
    return in_maps


def kernel(**inputs):
    in_maps = make_inputs(**inputs)
    nc = _get_nc()
    res = run_bass_kernel_spmd(nc, in_maps, core_ids=list(range(8)))
    out = np.zeros((2, T, C), np.float32)
    for c in range(8):
        b, s = c // 4, c % 4
        cfg = slice_cfg(s)
        o = res.results[c]["out"]
        off = cfg["T0"] - cfg["Q0"]
        out[b, cfg["T0"]:cfg["T1"]] = o[off: off + (cfg["T1"] - cfg["T0"])]
    if KDEBUG:
        kernel.last_results = res
    return out
